# version 12 (final)
# baseline (speedup 1.0000x reference)
import numpy as np
from contextlib import ExitStack

import concourse.bass as bass
import concourse.mybir as mybir
from concourse.bass_utils import run_bass_kernel_spmd

F32 = mybir.dt.float32
F32R = mybir.dt.float32r
BF16 = mybir.dt.bfloat16
AF = mybir.ActivationFunctionType
ALU = mybir.AluOpType
AX = mybir.AxisListType

D = 2048
DC = 16
T = 2048
B = 4
TOWN = 1024
HKV = 8
E = 16
HT1 = 11
HTS = 22
TOPK = 6
GOFF = 6.906768
EPS_LN = 1e-5
LOG_ISQD = -0.5 * float(np.log(128.0))

T_LO = 1024
KV = 2048
NSG = KV // 512
OWN_G0 = T_LO // 512
SCK = KV // 128

DEBUG = False

_OWN = {"sp": ("qS",), "pe": ("pe",), "act": ("act", "qA"),
        "dve": ("dve",), "pool": ("pool", "qG")}


class Ctr:
    def __init__(self, name, step=1):
        self.name = name
        self.step = step
        self.count = 0
        self.sem = None

    def bump(self):
        self.count += self.step
        return self.count


class Op:
    __slots__ = ("eng", "emit", "waits", "incs")

    def __init__(self, eng, emit, waits, incs):
        self.eng = eng
        self.emit = emit
        self.waits = waits
        self.incs = incs


class Builder:
    def __init__(self, nc):
        self.nc = nc
        self.ops = []
        self.ctr = {n: Ctr(n, 16 if n.startswith("q") else 1)
                    for n in ("qS", "qG", "qA", "pe", "act", "dve", "pool")}
        self.marks = {e: {} for e in _OWN}

    def op(self, eng, emit, reads=(), inc=None):
        m = self.marks[eng]
        waits = []
        for tok in reads:
            if tok is None:
                continue
            c, v = tok
            if c.name in _OWN[eng]:
                continue
            if m.get(c.name, 0) >= v:
                continue
            m[c.name] = v
            waits.append((c, v))
        incs = []
        tok = None
        if inc is not None:
            c = self.ctr[inc]
            v = c.bump()
            incs.append((c, v))
            tok = (c, v)
        self.ops.append(Op(eng, emit, waits, incs))
        return tok

    def fence(self, eng, tok):
        if tok is None:
            return
        c, v = tok
        self.marks[eng][c.name] = max(self.marks[eng].get(c.name, 0), v)
        self.ops.append(Op(eng, None, [(c, v)], []))

    def barrier(self, engines=("sp", "pe", "act", "dve", "pool")):
        toks = [(c, c.count) for c in self.ctr.values() if c.count > 0]
        for e in engines:
            m = self.marks[e]
            waits = []
            for c, v in toks:
                if c.name in _OWN[e] or m.get(c.name, 0) >= v:
                    continue
                m[c.name] = v
                waits.append((c, v))
            if waits:
                self.ops.append(Op(e, None, waits, []))

    def replay(self, block):
        prog = {e: [o for o in self.ops if o.eng == e] for e in _OWN}

        def body(engname):
            def f(eng):
                for o in prog[engname]:
                    for c, v in o.waits:
                        eng.wait_ge(c.sem, v)
                    if o.emit is not None:
                        ins = o.emit(eng)
                        for c, _v in o.incs:
                            ins.then_inc(c.sem, c.step)
            return f

        block.sync(body("sp"))
        block.tensor(body("pe"))
        block.scalar(body("act"))
        block.vector(body("dve"))
        block.gpsimd(body("pool"))


def _copy(eng_name):
    def g(e, dst, src):
        if eng_name == "act":
            return e.copy(dst, src)
        return e.tensor_copy(dst, src)
    return g


def _dma(dst_ap, src_ap):
    def f(eng):
        return eng.dma_start(out=dst_ap, in_=src_ap)
    return f


def build_core():
    nc = bass.Bass("TRN2", target_bir_lowering=False, debug=False)
    dram = {}

    def dr(name, shape, dtp, kind="ExternalInput"):
        dram[name] = nc.dram_tensor(name, list(shape), dtp, kind=kind).ap()
        return dram[name]

    x_in = dr("x", (KV, D), F32)
    wqp = dr("wqp", (16, 128, DC * 128), F32)
    wkp = dr("wkp", (8, 128, DC * 128), F32)
    wvp = dr("wvp", (2, 8, 128, 2 * 512), F32)
    wgp = dr("wgp", (128, DC * 8), F32)
    wcp = dr("wcp", (16, 128, DC * 128), F32)
    gwp = dr("gwp", (128, DC * 16), F32)
    w13p = dr("w13p", (E, HT1, 128, 2 * DC * 128), F32)
    w2p = dr("w2p", (E, 4, 128, HT1 * 512), F32)
    ws13p = dr("ws13p", (HTS, 128, 2 * DC * 128), F32)
    ws2p = dr("ws2p", (4, 128, HTS * 512), F32)
    b2m = dr("b2m", (16, D), F32)
    bcp = dr("bcp", (128, 16), F32)
    b1p = dr("b1p", (128, E * HT1), F32)
    b3p = dr("b3p", (128, E * HT1), F32)
    bs1p = dr("bs1p", (128, HTS), F32)
    bs3p = dr("bs3p", (128, HTS), F32)
    bs2p = dr("bs2p", (128, 16), F32)
    gbbp = dr("gbbp", (128, 16), F32)
    cvecp = dr("cvecp", (128, 8), F32)
    esel8 = dr("esel8", (8, 8 * 128), BF16)
    esel16 = dr("esel16", (16, 16 * 128), BF16)
    ones1 = dr("ones1", (1, 128), BF16)
    onescol = dr("onescol", (128, 1), BF16)
    masks = dr("masks", (128, 4 * 512), BF16)
    identf = dr("identf", (128, 128), F32)
    out = dr("out", (TOWN, D), F32, kind="ExternalOutput")
    if DEBUG:
        dr("dbg_xn", (128, DC * TOWN), BF16, kind="ExternalOutput")
        dr("dbg_logg", (8, KV), F32, kind="ExternalOutput")
        dr("dbg_yatt", (128, 16 * TOWN), BF16, kind="ExternalOutput")
        dr("dbg_ywc", (128, DC * TOWN), F32, kind="ExternalOutput")
        dr("dbg_cw", (16, TOWN), BF16, kind="ExternalOutput")
        dr("dbg_gfm", (8, KV), F32, kind="ExternalOutput")
        dr("dbg_lgp", (8, KV), F32, kind="ExternalOutput")

    bd = Builder(nc)
    es = ExitStack()

    def sb_(name, shape, dtp):
        return es.enter_context(nc.sbuf_tensor(name, list(shape), dtp))

    for c in bd.ctr.values():
        c.sem = es.enter_context(nc.semaphore(c.name))

    xn = sb_("xn", (128, DC * TOWN), BF16)
    logg = sb_("logg", (8, KV), F32)
    gfm = sb_("gfm", (8, KV), F32)
    cwfm = sb_("cwfm", (16, TOWN), BF16)
    cbias = sb_("cbias", (128, 16), F32)
    b1s = sb_("b1s", (128, E * HT1), F32)
    b3s = sb_("b3s", (128, E * HT1), F32)
    bs1s = sb_("bs1s", (128, HTS), F32)
    bs3s = sb_("bs3s", (128, HTS), F32)
    bs2s = sb_("bs2s", (128, 16), F32)
    b2s = sb_("b2s", (16, D), BF16)
    wgs = sb_("wgs", (128, DC * 8), BF16)
    gwsb = sb_("gwsb", (128, DC * 16), BF16)
    gbb = sb_("gbb", (128, 16), F32)
    cv = sb_("cv", (128, 8), F32)
    esl8 = sb_("esl8", (8, 8 * 128), BF16)
    esl16 = sb_("esl16", (16, 16 * 128), BF16)
    on1 = sb_("on1", (1, 128), BF16)
    onc = sb_("onc", (128, 1), BF16)
    idf = sb_("idf", (128, 128), F32)

    kvu = sb_("kvu", (128, 16384), F32)
    uni2 = sb_("uni2", (128, 24576), BF16)
    uni2f = uni2[:].bitcast(F32)
    kvb = kvu[:].bitcast(BF16)

    def KF(e_kv, a, b):
        return kvb[:, e_kv * KV + a: e_kv * KV + b]

    def VT(sc, a, b):
        return kvb[:, 16384 + sc * 1024 + a: 16384 + sc * 1024 + b]

    def YA(oc, a, b):
        return kvu[:, oc * TOWN + a: oc * TOWN + b]

    def QY(h, a, b):
        return uni2[:, h * TOWN + a: h * TOWN + b]

    def HS(hofs, a, b):
        return uni2[:, hofs * TOWN + a: hofs * TOWN + b]

    tcl = []
    for d_ap, s_ap in [
        (cbias[:], bcp[:]), (b1s[:], b1p[:]), (b3s[:], b3p[:]),
        (bs1s[:], bs1p[:]), (bs3s[:], bs3p[:]), (bs2s[:], bs2p[:]),
        (gbb[:], gbbp[:]), (cv[:], cvecp[:]),
        (esl8[:], esel8[:]), (esl16[:], esel16[:]),
        (on1[:], ones1[:]), (onc[:], onescol[:]),
        (idf[:], identf[:]),
    ]:
        tcl.append(bd.op("act", _dma(d_ap, s_ap), inc="qA"))
    t_gbb = tcl[6]
    for d_ap, s_ap in [(wgs[:], wgp[:]), (gwsb[:], gwp[:]), (b2s[:], b2m[:])]:
        tcl.append(bd.op("pool", _dma(d_ap, s_ap), inc="qG"))
    tok_const = tcl[-4]
    tok_constg = tcl[-1]

    sC = ExitStack()

    def csb(name, shape, dtp):
        return sC.enter_context(nc.sbuf_tensor(name, list(shape), dtp))

    def cps(name, shape):
        return sC.enter_context(nc.psum_tensor(name, list(shape), F32))

    xti = uni2f[:, 8192:10240]
    xsq = uni2f[:, 10240:12288]
    st = csb("st", (128, 8), F32)
    xng = csb("xng", (128, DC * 512), BF16)
    wk_b = [csb(f"wk_b{i}", (128, DC * 128), BF16) for i in range(2)]
    wv_b = [csb(f"wv_b{i}", (128, 2 * 512), BF16) for i in range(2)]
    kps = [cps(f"kps{i}", (128, 512)) for i in range(2)]
    vps = [cps(f"vps{i}", (128, 512)) for i in range(4)]
    gps = cps("gps", (8, 512))

    free = {}
    own_xn_toks = []
    for g in range(NSG):
        own = g >= OWN_G0
        grp_xn = []
        for ti in range(4):
            tc = g * 4 + ti
            t_x = bd.op("sp", _dma(xti, x_in[tc * 128:(tc + 1) * 128, :]),
                        reads=[free.get("xtiread")], inc="qS")
            t_s = bd.op("dve", lambda e: e.reduce_sum(st[:, 0:1], xti, axis=AX.X),
                        reads=[t_x, free.get("stread")], inc="dve")
            bd.fence("dve", t_s)
            t_q = bd.op("act", lambda e: e.activation(
                xsq, xti, AF.Square, accum_out=st[:, 1:2]),
                reads=[t_x], inc="act")
            t1 = bd.op("dve", lambda e: e.tensor_scalar(
                st[:, 2:3], st[:, 0:1], 1.0 / D, None, ALU.mult), inc="dve")
            bd.fence("dve", t1)
            t2 = bd.op("dve", lambda e: e.tensor_scalar(
                st[:, 4:5], st[:, 1:2], 1.0 / D, None, ALU.mult),
                reads=[t_q], inc="dve")
            bd.fence("dve", t2)
            t3 = bd.op("dve", lambda e: e.scalar_tensor_tensor(
                st[:, 3:4], st[:, 2:3], -1.0, st[:, 2:3], ALU.mult, ALU.mult), inc="dve")
            bd.fence("dve", t3)
            t4 = bd.op("dve", lambda e: e.tensor_add(st[:, 4:5], st[:, 4:5], st[:, 3:4]),
                       inc="dve")
            bd.fence("dve", t4)
            t5 = bd.op("act", lambda e: e.activation(
                st[:, 5:6], st[:, 4:5], AF.Sqrt, bias=cv[:, 0:1]),
                reads=[t4, tok_const], inc="act")
            bd.fence("act", t5)
            t6 = bd.op("dve", lambda e: e.reciprocal(st[:, 6:7], st[:, 5:6]),
                       reads=[t5], inc="dve")
            bd.fence("dve", t6)
            t7 = bd.op("dve", lambda e: e.tensor_scalar(
                st[:, 7:8], st[:, 2:3], st[:, 6:7], -1.0, ALU.mult, ALU.mult), inc="dve")
            bd.fence("dve", t7)
            t_n = bd.op("dve", lambda e: e.tensor_scalar(
                xti, xti, st[:, 6:7], st[:, 7:8], ALU.mult, ALU.add), inc="dve")
            bd.fence("dve", t_n)
            free["stread"] = t_n
            if own:
                tloc = tc * 128 - T_LO
                dst = lambda dc, tloc=tloc: xn[:, dc * TOWN + tloc: dc * TOWN + tloc + 128]
            else:
                dst = lambda dc, ti=ti: xng[:, dc * 512 + ti * 128: dc * 512 + (ti + 1) * 128]
            last_tr = None
            last_ev = None
            for dc in range(DC):
                pbi = dc % 4
                t_tr = bd.op("pe", lambda e, dc=dc, pbi=pbi: e.transpose(
                    vps[pbi][0:128, 0:128], xti[:, dc * 128:(dc + 1) * 128], idf[:]),
                    reads=[t_n, tok_const, free.get(("vbank", pbi))], inc="pe")
                ev_eng = "dve" if dc % 2 == 0 else "act"
                t_ev = bd.op(ev_eng, lambda e, dc=dc, pbi=pbi, dst=dst, ev_eng=ev_eng:
                    _copy(ev_eng)(e, dst(dc), vps[pbi][0:128, 0:128]),
                    reads=[t_tr, free.get("xngread") if not own else None], inc=ev_eng)
                free[("vbank", pbi)] = t_ev
                last_tr = t_tr
                last_ev = t_ev
            free["xtiread"] = last_tr
            grp_xn.append(last_ev)
        if own:
            own_xn_toks += grp_xn

        if own:
            tloc = g * 512 - T_LO
            src = lambda dc, tloc=tloc: xn[:, dc * TOWN + tloc: dc * TOWN + tloc + 512]
        else:
            src = lambda dc: xng[:, dc * 512:(dc + 1) * 512]

        for kc in range(8):
            bi = kc % 2
            t_w = bd.op("pool", _dma(wk_b[bi][:], wkp[kc]),
                        reads=[free.get(("wkread", bi))], inc="qG")
            t_mm = None
            for dc in range(DC):
                rd = ([t_w, free.get(("kbank", bi))] + grp_xn) if dc == 0 else []
                t_mm = bd.op("pe", lambda e, bi=bi, dc=dc, src=src: e.matmul(
                    kps[bi][:], wk_b[bi][:, dc * 128:(dc + 1) * 128], src(dc),
                    start=(dc == 0), stop=(dc == DC - 1)),
                    reads=rd, inc="pe" if dc == DC - 1 else None)
            free[("wkread", bi)] = t_mm
            keng = "dve" if kc % 2 == 0 else "act"
            t_ev = bd.op(keng, lambda e, bi=bi, kc=kc, g=g, keng=keng: _copy(keng)(
                e, KF(kc, g * 512, (g + 1) * 512), kps[bi][:]),
                reads=[t_mm], inc=keng)
            free[("kbank", bi)] = t_ev

        for vh in range(2):
            t_last = [None] * 4
            for dcq in range(8):
                bi = dcq % 2
                t_w = bd.op("pool", _dma(wv_b[bi][:], wvp[vh, dcq]),
                            reads=[free.get(("wvread", bi))], inc="qG")
                for dci in range(2):
                    dc = dcq * 2 + dci
                    for sc in range(4):
                        rd = [t_w] + ([free.get(("vbank", sc)), grp_xn[sc]]
                                      if dc == 0 else [])
                        t_last[sc] = bd.op("pe",
                            lambda e, bi=bi, dc=dc, dci=dci, sc=sc, src=src: e.matmul(
                                vps[sc][:], src(dc)[:, sc * 128:(sc + 1) * 128],
                                wv_b[bi][:, dci * 512:(dci + 1) * 512],
                                start=(dc == 0), stop=(dc == DC - 1)),
                            reads=rd,
                            inc="pe" if (dc == DC - 1 or sc == 3) else None)
                free[("wvread", bi)] = t_last[3]
            for sc in range(4):
                eng = "dve" if sc % 2 == 0 else "act"
                t_ev = bd.op(eng, lambda e, sc=sc, g=g, vh=vh, eng=eng: _copy(eng)(
                    e, VT(g * 4 + sc, vh * 512, (vh + 1) * 512), vps[sc][:]),
                    reads=[t_last[sc]], inc=eng)
                free[("vbank", sc)] = t_ev

        t_mm = None
        for dc in range(DC):
            rd = ([tok_constg, free.get("gpsread")] + grp_xn) if dc == 0 else []
            t_mm = bd.op("pe", lambda e, dc=dc, src=src: e.matmul(
                gps[:], wgs[:, dc * 8:(dc + 1) * 8], src(dc),
                start=(dc == 0), stop=(dc == DC - 1)),
                reads=rd, inc="pe" if dc == DC - 1 else None)
        t_ev = bd.op("dve", lambda e, g=g: e.tensor_copy(
            gfm[:, g * 512:(g + 1) * 512], gps[:]), reads=[t_mm], inc="dve")
        free["gpsread"] = t_ev
        free[("g_ev", g)] = t_ev
        if not own:
            free["xngread"] = t_mm

    for qc in range(16):
        bi = qc % 2
        t_w = bd.op("pool", _dma(wk_b[bi][:], wqp[qc]),
                    reads=[free.get(("wkread", bi))], inc="qG")
        for th in range(2):
            pb = th
            t_mm = None
            for dc in range(DC):
                rd = ([t_w, free.get(("kbank", pb))] + own_xn_toks) if dc == 0 else []
                t_mm = bd.op("pe", lambda e, bi=bi, dc=dc, th=th, pb=pb: e.matmul(
                    kps[pb][:], wk_b[bi][:, dc * 128:(dc + 1) * 128],
                    xn[:, dc * TOWN + th * 512: dc * TOWN + (th + 1) * 512],
                    start=(dc == 0), stop=(dc == DC - 1)),
                    reads=rd, inc="pe" if dc == DC - 1 else None)
            if th == 1:
                free[("wkread", bi)] = t_mm
            qeng = "dve" if qc % 2 == 0 else "act"
            t_ev = bd.op(qeng, lambda e, pb=pb, qc=qc, th=th, qeng=qeng: _copy(qeng)(
                e, QY(qc, th * 512, (th + 1) * 512), kps[pb][:]),
                reads=[t_mm], inc=qeng)
            free[("kbank", pb)] = t_ev

    g_toks = [free[("g_ev", g)] for g in range(NSG)]
    az = xsq
    t_az = bd.op("act", lambda e: e.activation(
        az[0:8, 0:KV], gfm[:], AF.Abs, bias=cv[0:8, 1:2]),
        reads=g_toks + [tok_const], inc="act")
    bd.fence("act", t_az)
    t_e1 = bd.op("act", lambda e: e.activation(
        az[0:8, 0:KV], az[0:8, 0:KV], AF.Exp, scale=cv[0:8, 3:4]), inc="act")
    bd.fence("act", t_e1)
    t_l = bd.op("act", lambda e: e.activation(
        az[0:8, 0:KV], az[0:8, 0:KV], AF.Ln, bias=cv[0:8, 2:3]), inc="act")
    t_mz = bd.op("dve", lambda e: e.tensor_scalar(
        logg[:], gfm[:], GOFF, 0.0, ALU.add, ALU.min), reads=g_toks, inc="dve")
    bd.fence("dve", t_mz)
    t_lg = bd.op("dve", lambda e: e.tensor_sub(logg[:], logg[:], az[0:8, 0:KV]),
                 reads=[t_l], inc="dve")
    bd.fence("dve", t_lg)
    t_dlg = None
    if DEBUG:
        bd.op("sp", _dma(dram["dbg_gfm"][:], gfm[:]), reads=g_toks, inc="qS")
        t_dlg = bd.op("sp", _dma(dram["dbg_lgp"][:], logg[:]), reads=[t_lg], inc="qS")
    t_z = bd.op("dve", lambda e: e.memset(az[0:8, 0:KV], 0.0), inc="dve")
    bd.fence("dve", t_z)
    t_logg = bd.op("dve", lambda e: e.tensor_tensor_scan(
        logg[:], logg[:], az[0:8, 0:KV], 0.0, ALU.add, ALU.add),
        reads=[t_dlg], inc="dve")
    bd.fence("dve", t_logg)
    if DEBUG:
        bd.op("sp", _dma(dram["dbg_logg"][:], logg[:]), reads=[t_logg], inc="qS")
        bd.op("sp", _dma(dram["dbg_xn"][:], xn[:]), reads=own_xn_toks, inc="qS")

    sC.close()
    bd.barrier()

    sD = ExitStack()

    def dsb(name, shape, dtp):
        return sD.enter_context(nc.sbuf_tensor(name, list(shape), dtp))

    def dps(name, shape):
        return sD.enter_context(nc.psum_tensor(name, list(shape), F32))

    msk = uni2[:, 16384:18432]
    t_msk = bd.op("act", _dma(msk, masks[:]), inc="qA")
    u8 = dsb("u8", (8, 512), BF16)
    u8s = dsb("u8s", (8, 512), F32)
    wfm = dsb("wfm", (8, KV), F32)
    wtm = dsb("wtm", (128, SCK * 8), F32)
    ub = dsb("ub", (128, 5 * 512), BF16)
    a1 = [dsb(f"a1_{i}", (128, 512), BF16) for i in range(3)]
    asb = [dsb(f"asb{i}", (128, 512), BF16) for i in range(3)]
    nrm = dsb("nrm", (1, 512), F32)
    rcp = dsb("rcp", (1, 512), F32)
    rcpb = dsb("rcpb", (1, 512), BF16)
    rbs = dsb("rbs", (128, 512), BF16)
    qk_ps = [dps(f"qk_ps{i}", (128, 512)) for i in range(3)]
    y_ps = [dps(f"y_ps{i}", (128, 512)) for i in range(2)]
    n_ps = [dps(f"n_ps{i}", (128, 512)) for i in range(2)]
    aux_ps = dps("aux_ps", (128, 512))

    free = {}
    for tb in range(2):
        t0g = T_LO + tb * 512
        nsc = (t0g + 512) // 128
        t_u8s = bd.op("dve", lambda e, t0g=t0g: e.tensor_scalar(
            u8s[:], logg[:, t0g:t0g + 512], logg[:, t0g:t0g + 1], None, ALU.subtract),
            reads=[t_logg, free.get("u8read")], inc="dve")
        t_u8 = bd.op("act", lambda e: e.activation(u8[:], u8s[:], AF.Exp),
                     reads=[t_u8s], inc="act")
        t_wf = bd.op("dve", lambda e, t0g=t0g, nsc=nsc: e.tensor_scalar(
            wfm[:, 0:nsc * 128], logg[:, 0:nsc * 128], logg[:, t0g:t0g + 1], None,
            ALU.subtract),
            reads=[t_logg, free.get("wfmread")], inc="dve")
        t_wfe = bd.op("act", lambda e, nsc=nsc: e.activation(
            wfm[:, 0:nsc * 128], wfm[:, 0:nsc * 128], AF.Exp,
            scale=cv[0:8, 4:5], bias=cv[0:8, 5:6]),
            reads=[t_wf, tok_const], inc="act")
        bd.fence("act", t_wfe)
        t_wt_all = []
        for sc in range(nsc):
            pbi = sc % 2
            t_tr = bd.op("pe", lambda e, sc=sc, pbi=pbi: e.transpose(
                qk_ps[pbi][0:128, 0:8], wfm[:, sc * 128:(sc + 1) * 128], idf[0:8, 0:8]),
                reads=[t_wfe, free.get(("qkread", pbi))], inc="pe")
            t_ev = bd.op("dve", lambda e, sc=sc, pbi=pbi: e.tensor_copy(
                wtm[:, sc * 8:(sc + 1) * 8], qk_ps[pbi][0:128, 0:8]),
                reads=[t_tr, free.get("wtmread")], inc="dve")
            free[("qkread", pbi)] = t_ev
            t_wt_all.append(t_ev)
        free["wfmread"] = t_wt_all[-1]

        for e_kv in range(8):
            t_sel = bd.op("pe", lambda e, e_kv=e_kv: e.matmul(
                aux_ps[:], esl8[:, e_kv * 128:(e_kv + 1) * 128], u8[:],
                start=True, stop=True),
                reads=[t_u8, tok_const, free.get("auxread")], inc="pe")
            t_ub = bd.op("dve", lambda e: e.tensor_copy(ub[:, 4 * 512:5 * 512], aux_ps[:]),
                         reads=[t_sel], inc="dve")
            free["auxread"] = t_ub
            t_ubm = [None] * 4
            for o in range(4):
                t_ubm[o] = bd.op("dve", lambda e, o=o: e.tensor_mul(
                    ub[:, o * 512:(o + 1) * 512], ub[:, 4 * 512:5 * 512],
                    msk[:, o * 512:(o + 1) * 512]),
                    reads=[t_msk], inc="dve")
            bd.fence("dve", t_ubm[3])
            t_y_stop = [None, None]
            t_n_stop = [None, None]
            NU = 2 * nsc

            def unit(n):
                sc, hh = n // 2, n % 2
                diag = sc * 128 >= t0g
                v_ub = (sc * 128 - t0g) // 128 if diag else 4
                return sc, hh, v_ub, (t_ubm[v_ub] if diag else t_ub)

            def emit_qk(n):
                sc, hh, v_ub, t_ubv = unit(n)
                h = 2 * e_kv + hh
                qbi = n % 3
                rd = [free.get(("qkread", qbi))]
                if n == 0:
                    rd += [free.get(("yread", 0)), free.get(("yread", 1)),
                           free.get(("nread", 0)), free.get(("nread", 1))]
                return bd.op("pe", lambda e, e_kv=e_kv, sc=sc, h=h, tb=tb, qbi=qbi: e.matmul(
                    qk_ps[qbi][:], KF(e_kv, sc * 128, (sc + 1) * 128),
                    QY(h, tb * 512, (tb + 1) * 512),
                    start=True, stop=True),
                    reads=rd, inc="pe")

            t_qk_l = [None] * NU
            for n in range(min(3, NU)):
                t_qk_l[n] = emit_qk(n)
            for n in range(NU):
                sc, hh, v_ub, t_ubv = unit(n)
                qbi = n % 3
                t_a1 = bd.op("act", lambda e, sc=sc, e_kv=e_kv, qbi=qbi: e.activation(
                    a1[qbi][:], qk_ps[qbi][:], AF.Square,
                    scale=wtm[:, sc * 8 + e_kv: sc * 8 + e_kv + 1]),
                    reads=[t_qk_l[n], t_wt_all[sc], free.get(("a1read", qbi))], inc="act")
                free[("qkread", qbi)] = t_a1
                t_a = bd.op("dve", lambda e, qbi=qbi, v_ub=v_ub: e.tensor_mul(
                    asb[qbi][:], a1[qbi][:], ub[:, v_ub * 512:(v_ub + 1) * 512]),
                    reads=[t_a1, t_ubv, free.get(("aread", qbi))], inc="dve")
                free[("a1read", qbi)] = t_a
                t_pv = bd.op("pe", lambda e, sc=sc, e_kv=e_kv, hh=hh, qbi=qbi, nsc=nsc: e.matmul(
                    y_ps[hh][:], VT(sc, e_kv * 128, (e_kv + 1) * 128),
                    asb[qbi][:], start=(sc == 0), stop=(sc == nsc - 1)),
                    reads=[t_a], inc="pe" if sc == nsc - 1 else None)
                t_nm = bd.op("pe", lambda e, hh=hh, qbi=qbi, sc=sc, nsc=nsc: e.matmul(
                    n_ps[hh][0:1, :], onc[:], asb[qbi][:],
                    start=(sc == 0), stop=(sc == nsc - 1)),
                    reads=[], inc="pe" if sc == nsc - 1 else None)
                free[("aread", qbi)] = t_nm if t_nm is not None else t_pv
                if n + 3 < NU:
                    t_qk_l[n + 3] = emit_qk(n + 3)
                if sc == nsc - 1:
                    t_y_stop[hh] = t_pv
                    t_n_stop[hh] = t_nm
            for hh in range(2):
                h = 2 * e_kv + hh
                t_na = bd.op("dve", lambda e, hh=hh: e.tensor_scalar(
                    nrm[:], n_ps[hh][0:1, :], 1e-6, None, ALU.add),
                    reads=[t_n_stop[hh]], inc="dve")
                bd.fence("dve", t_na)
                t_rc = bd.op("act", lambda e: e.activation(rcp[:], nrm[:], AF.Ln),
                             reads=[t_na, free.get("rcpread")], inc="act")
                bd.fence("act", t_rc)
                t_rcb = bd.op("act", lambda e: e.activation(
                    rcpb[:], rcp[:], AF.Exp, scale=cv[0:1, 3:4]), inc="act")
                bd.fence("act", t_rcb)
                t_bc = bd.op("pe", lambda e, hh=hh: e.matmul(
                    n_ps[hh][:], on1[:], rcpb[:], start=True, stop=True),
                    reads=[t_rcb, t_na], inc="pe")
                free["rcpread"] = t_bc
                t_rb = bd.op("dve", lambda e, hh=hh: e.tensor_copy(rbs[:], n_ps[hh][:]),
                             reads=[t_bc], inc="dve")
                bd.fence("dve", t_rb)
                free[("nread", hh)] = t_rb
                t_y = bd.op("dve", lambda e, h=h, tb=tb, hh=hh: e.tensor_mul(
                    QY(h, tb * 512, (tb + 1) * 512), y_ps[hh][:], rbs[:]),
                    reads=[t_y_stop[hh]], inc="dve")
                free[("yread", hh)] = t_y
        free["u8read"] = free[("yread", 1)]
        free["wtmread"] = free[("a1read", 1)]
    t_yatt = [free[("yread", 0)], free[("yread", 1)]]
    if DEBUG:
        bd.op("sp", _dma(dram["dbg_yatt"][:], uni2[:, 0:16 * TOWN]),
              reads=t_yatt, inc="qS")

    sD.close()
    bd.barrier()

    sE = ExitStack()

    def esb(name, shape, dtp):
        return sE.enter_context(nc.sbuf_tensor(name, list(shape), dtp))

    def eps(name, shape):
        return sE.enter_context(nc.psum_tensor(name, list(shape), F32))

    sE1 = ExitStack()
    wc_b = [sE1.enter_context(nc.sbuf_tensor(f"wc_b{i}", [128, DC * 128], BF16))
            for i in range(2)]
    c_ps = [eps(f"c_ps{i}", (128, 512)) for i in range(2)]
    free = {}
    for oc in range(16):
        bi = oc % 2
        t_w = bd.op("pool", _dma(wc_b[bi][:], wcp[oc]),
                    reads=[free.get(("wcread", bi))], inc="qG")
        for th in range(2):
            pb = th
            t_mm = None
            for dc in range(DC):
                rd = ([t_w, free.get(("cbank", pb))] + t_yatt) if dc == 0 else []
                t_mm = bd.op("pe", lambda e, bi=bi, dc=dc, th=th, pb=pb: e.matmul(
                    c_ps[pb][:], wc_b[bi][:, dc * 128:(dc + 1) * 128],
                    QY(dc, th * 512, (th + 1) * 512),
                    start=(dc == 0), stop=(dc == DC - 1)),
                    reads=rd, inc="pe" if dc == DC - 1 else None)
            if th == 1:
                free[("wcread", bi)] = t_mm
            t_ev = bd.op("dve", lambda e, pb=pb, oc=oc, th=th: e.tensor_scalar(
                YA(oc, th * 512, (th + 1) * 512),
                c_ps[pb][:], cbias[:, oc:oc + 1], None, ALU.add),
                reads=[t_mm, tok_const], inc="dve")
            free[("cbank", pb)] = t_ev
    t_ywc = [free[("cbank", 0)], free[("cbank", 1)]]
    if DEBUG:
        bd.op("sp", _dma(dram["dbg_ywc"][:], kvu[:]), reads=t_ywc, inc="qS")

    bd.barrier()
    sE1.close()
    r_ps = c_ps[0]
    prb = esb("prb", (128, 64), F32)
    rst = esb("rst", (128, 8), F32)
    cw_tm = esb("cw_tm", (128, 16), F32)
    free["rps_war"] = t_ywc[0]
    t_cw_tr = []
    for tcn in range(8):
        t_mm = None
        for dc in range(DC):
            rd = ([free.get("rps_war"), free.get("rps_war2")] + own_xn_toks) \
                if dc == 0 else []
            t_mm = bd.op("pe", lambda e, dc=dc, tcn=tcn: e.matmul(
                r_ps[0:128, 16:32],
                xn[:, dc * TOWN + tcn * 128: dc * TOWN + (tcn + 1) * 128],
                gwsb[:, dc * 16:(dc + 1) * 16],
                start=(dc == 0), stop=(dc == DC - 1)),
                reads=rd, inc="pe" if dc == DC - 1 else None)
        t_sc = t_mm
        t_mx = bd.op("dve", lambda e: e.reduce_max(
            rst[:, 0:1], r_ps[0:128, 16:32], axis=AX.X),
            reads=[t_sc, free.get("rstread")], inc="dve")
        bd.fence("dve", t_mx)
        t_sub = bd.op("dve", lambda e: e.tensor_scalar(
            prb[:, 48:64], r_ps[0:128, 16:32], rst[:, 0:1], None, ALU.subtract),
            reads=[free.get("prbread")], inc="dve")
        bd.fence("dve", t_sub)
        free["rps_war"] = t_sub
        t_ex = bd.op("act", lambda e: e.activation(prb[:, 0:16], prb[:, 48:64], AF.Exp),
                     reads=[t_sub], inc="act")
        t_sm = bd.op("dve", lambda e: e.reduce_sum(rst[:, 1:2], prb[:, 0:16], axis=AX.X),
                     reads=[t_ex], inc="dve")
        bd.fence("dve", t_sm)
        t_rc = bd.op("dve", lambda e: e.reciprocal(rst[:, 2:3], rst[:, 1:2]), inc="dve")
        bd.fence("dve", t_rc)
        t_pr = bd.op("dve", lambda e: e.tensor_scalar(
            prb[:, 0:16], prb[:, 0:16], rst[:, 2:3], None, ALU.mult), inc="dve")
        bd.fence("dve", t_pr)
        t_bi = bd.op("dve", lambda e: e.tensor_add(prb[:, 16:32], prb[:, 0:16], gbb[:]),
                     reads=[t_gbb], inc="dve")
        bd.fence("dve", t_bi)
        last_cw = None
        for ee in range(E):
            t_gt = bd.op("dve", lambda e, ee=ee: e.tensor_scalar(
                prb[:, 48:64], prb[:, 16:32], prb[:, 16 + ee:17 + ee], 0.0,
                ALU.is_gt, ALU.add, accum_out=rst[:, 3:4]), inc="dve")
            bd.fence("dve", t_gt)
            if ee > 0:
                t_eq = bd.op("dve", lambda e, ee=ee: e.tensor_scalar(
                    prb[:, 48:48 + ee], prb[:, 16:16 + ee], prb[:, 16 + ee:17 + ee], 0.0,
                    ALU.is_equal, ALU.add, accum_out=rst[:, 4:5]), inc="dve")
                bd.fence("dve", t_eq)
                t_rk = bd.op("dve", lambda e: e.tensor_add(
                    rst[:, 3:4], rst[:, 3:4], rst[:, 4:5]), inc="dve")
                bd.fence("dve", t_rk)
            t_ms = bd.op("dve", lambda e: e.tensor_scalar(
                rst[:, 5:6], rst[:, 3:4], float(TOPK) - 0.5, None, ALU.is_lt), inc="dve")
            bd.fence("dve", t_ms)
            last_cw = bd.op("dve", lambda e, ee=ee: e.tensor_scalar(
                cw_tm[:, ee:ee + 1], prb[:, ee:ee + 1], rst[:, 5:6], None, ALU.mult),
                reads=[free.get("cwtmread")], inc="dve")
            bd.fence("dve", last_cw)
        free["prbread"] = last_cw
        free["rstread"] = last_cw
        t_tr = bd.op("pe", lambda e: e.transpose(r_ps[0:16, 32:160], cw_tm[:], idf[:]),
                     reads=[last_cw, free.get("rps_war3")], inc="pe")
        free["cwtmread"] = t_tr
        t_cf = bd.op("dve", lambda e, tcn=tcn: e.tensor_copy(
            cwfm[0:16, tcn * 128:(tcn + 1) * 128], r_ps[0:16, 32:160]),
            reads=[t_tr], inc="dve")
        free["rps_war3"] = t_cf
        free["rps_war2"] = t_cf
        t_cw_tr.append(t_cf)
    if DEBUG:
        bd.op("sp", _dma(dram["dbg_cw"][:], cwfm[:]),
              reads=t_cw_tr, inc="qS")

    w13_b = [esb(f"w13_{i}", (128, 2 * DC * 128), BF16) for i in range(2)]
    w2_b = [esb(f"w2_{i}", (128, 4 * 512), BF16) for i in range(3)]
    ssb = [esb(f"ssb{i}", (128, 512), F32)[:] for i in range(2)]
    cwb = uni2[:, 22528:24576]
    h_ps = [eps(f"h_ps{i}", (128, 512)) for i in range(2)]
    w2_ps = [eps(f"w2_ps{i}", (128, 512)) for i in range(4)]

    state = {"cnt": 0, "scnt": 0, "w2cnt": 0}
    for k in range(HTS):
        free[("hread", k)] = t_ywc[1]

    def h_chunk(lhs1, b1col, b3col, cw_j, hofs):
        bi1 = state["cnt"] % 2
        state["cnt"] += 1
        t_w1 = bd.op("pool", _dma(w13_b[bi1][:], lhs1),
                     reads=[free.get(("w13read", bi1))], inc="qG")
        toks = []
        for tt in range(2):
            t_m1 = None
            for dc in range(DC):
                rd = ([t_w1, free.get(("hbank", 0))] + own_xn_toks) if dc == 0 else []
                t_m1 = bd.op("pe", lambda e, dc=dc, bi1=bi1, tt=tt: e.matmul(
                    h_ps[0][:], w13_b[bi1][:, dc * 128:(dc + 1) * 128],
                    xn[:, dc * TOWN + tt * 512: dc * TOWN + (tt + 1) * 512],
                    start=(dc == 0), stop=(dc == DC - 1)),
                    reads=rd, inc="pe" if dc == DC - 1 else None)
            t_m3 = None
            for dc in range(DC):
                rd = [t_w1, free.get(("hbank", 1))] if dc == 0 else []
                t_m3 = bd.op("pe", lambda e, dc=dc, bi1=bi1, tt=tt: e.matmul(
                    h_ps[1][:], w13_b[bi1][:, DC * 128 + dc * 128: DC * 128 + (dc + 1) * 128],
                    xn[:, dc * TOWN + tt * 512: dc * TOWN + (tt + 1) * 512],
                    start=(dc == 0), stop=(dc == DC - 1)),
                    reads=rd, inc="pe" if dc == DC - 1 else None)
            if tt == 1:
                free[("w13read", bi1)] = t_m3
            sbi = state["scnt"] % 2
            state["scnt"] += 1
            t_si = bd.op("act", lambda e, sbi=sbi, b1col=b1col: e.activation(
                ssb[sbi], h_ps[0][:], AF.Silu, bias=b1col),
                reads=[t_m1, free.get(("ssbread", sbi))], inc="act")
            free[("hbank", 0)] = t_si
            if cw_j is not None:
                t_cw = bd.op("dve", lambda e, sbi=sbi, cw_j=cw_j, tt=tt: e.tensor_mul(
                    ssb[sbi], ssb[sbi],
                    cwb[:, cw_j * 1024 + tt * 512: cw_j * 1024 + (tt + 1) * 512]),
                    reads=[t_si, free.get(("cwbread", cw_j))], inc="dve")
                bd.fence("dve", t_cw)
                t_s = t_cw
            else:
                t_s = t_si
            t_h = bd.op("dve", lambda e, sbi=sbi, b3col=b3col, hofs=hofs, tt=tt:
                        e.scalar_tensor_tensor(
                            HS(hofs, tt * 512, (tt + 1) * 512), h_ps[1][:], b3col,
                            ssb[sbi], ALU.add, ALU.mult),
                        reads=[t_m3, t_s, free.get(("hread", hofs))], inc="dve")
            free[("hbank", 1)] = t_h
            free[("ssbread", sbi)] = t_h
            toks.append(t_h)
        free[("hread", hofs)] = None
        return toks

    def w2_pass(members, nht, w2_dram, bs2_add, extra_mm):
        for dg in range(4):
            for tt in range(2):
                t_stop = [None] * 4
                mi = 0
                if extra_mm:
                    for di in range(4):
                        t_stop[di] = bd.op("pe", lambda e, dg=dg, di=di, tt=tt: e.matmul(
                            w2_ps[di][:],
                            b2s[:, (dg * 4 + di) * 128:(dg * 4 + di + 1) * 128],
                            cwfm[:, tt * 512:(tt + 1) * 512], start=True, stop=False),
                            reads=[free.get(("w2bank", di))] + t_cw_tr, inc=None)
                    mi = 1
                for j, (e_idx, h_toks) in enumerate(members):
                    for ht0 in range(0, nht, 4):
                        ht1 = min(ht0 + 4, nht)
                        bi = state["w2cnt"] % 3
                        state["w2cnt"] += 1
                        t_w = bd.op("pool", _dma(
                            w2_b[bi][:, 0:(ht1 - ht0) * 512],
                            w2_dram(e_idx, dg)[:, ht0 * 512:ht1 * 512]),
                            reads=[free.get(("w2read", bi))], inc="qG")
                        for ht in range(ht0, ht1):
                            last_grp = (j == len(members) - 1) and (ht == nht - 1)
                            for di in range(4):
                                rd = [t_w, h_toks[ht][tt]]
                                if mi == 0:
                                    rd.append(free.get(("w2bank", di)))
                                t_stop[di] = bd.op("pe",
                                    lambda e, bi=bi, di=di, j=j, ht=ht, ht0=ht0, mi=mi,
                                           nht=nht, tt=tt, last_grp=last_grp: e.matmul(
                                        w2_ps[di][:],
                                        w2_b[bi][:, (ht - ht0) * 512 + di * 128:
                                                 (ht - ht0) * 512 + (di + 1) * 128],
                                        HS(j * nht + ht, tt * 512, (tt + 1) * 512),
                                        start=(mi == 0), stop=last_grp),
                                    reads=rd,
                                    inc="pe" if (last_grp or di == 3) else None)
                            mi += 1
                        free[("w2read", bi)] = t_stop[3]
                for di in range(4):
                    dchunk = dg * 4 + di
                    ysl = YA(dchunk, tt * 512, (tt + 1) * 512)
                    if bs2_add:
                        t_ev = bd.op("dve", lambda e, di=di, ysl=ysl, dchunk=dchunk:
                                     e.scalar_tensor_tensor(
                                         ysl, w2_ps[di][:], bs2s[:, dchunk:dchunk + 1],
                                         ysl, ALU.add, ALU.add),
                                     reads=[t_stop[di]] + t_ywc, inc="dve")
                    else:
                        t_ev = bd.op("dve", lambda e, di=di, ysl=ysl: e.tensor_add(
                            ysl, w2_ps[di][:], ysl),
                            reads=[t_stop[di]] + t_ywc, inc="dve")
                    free[("w2bank", di)] = t_ev

    for grp in range(8):
        members = []
        for j in range(2):
            e_idx = grp * 2 + j
            t_cwb = None
            for tt in range(2):
                t_sel = bd.op("pe", lambda e, e_idx=e_idx, tt=tt: e.matmul(
                    h_ps[0][:], esl16[:, e_idx * 128:(e_idx + 1) * 128],
                    cwfm[:, tt * 512:(tt + 1) * 512], start=True, stop=True),
                    reads=[tok_const, free.get(("hbank", 0))] + t_cw_tr[4 * tt:4 * tt + 4],
                    inc="pe")
                t_cwb = bd.op("dve", lambda e, j=j, tt=tt: e.tensor_copy(
                    cwb[:, j * 1024 + tt * 512: j * 1024 + (tt + 1) * 512], h_ps[0][:]),
                    reads=[t_sel, free.get(("cwbread", j))], inc="dve")
                free[("hbank", 0)] = t_cwb
                bd.fence("dve", t_cwb)
            free[("cwbread", j)] = t_cwb
            h_toks = []
            for ht in range(HT1):
                hofs = j * HT1 + ht
                toks = h_chunk(w13p[e_idx, ht],
                               b1s[:, e_idx * HT1 + ht: e_idx * HT1 + ht + 1],
                               b3s[:, e_idx * HT1 + ht: e_idx * HT1 + ht + 1],
                               j, hofs)
                h_toks.append(toks)
            free[("cwbread", j)] = h_toks[-1][-1]
            members.append((e_idx, h_toks))
        w2_pass(members, HT1, lambda e_idx, dg: w2p[e_idx, dg],
                bs2_add=False, extra_mm=False)
        for k in range(2 * HT1):
            free[("hread", k)] = free[("w2bank", 3)]

    h_toks = []
    for ht in range(HTS):
        toks = h_chunk(ws13p[ht], bs1s[:, ht:ht + 1], bs3s[:, ht:ht + 1],
                       None, ht)
        h_toks.append(toks)
    w2_pass([(0, h_toks)], HTS, lambda e_idx, dg: ws2p[dg],
            bs2_add=True, extra_mm=True)

    t_moe = [free[("w2bank", di)] for di in range(4)]
    sE.close()
    bd.barrier()

    sF = ExitStack()
    xep = [uni2f[:, 8192:10240], uni2f[:, 10240:12288]]
    osb = [sF.enter_context(nc.sbuf_tensor(f"osb{i}", [128, D], F32)) for i in range(2)]
    eps_ = [sF.enter_context(nc.psum_tensor(f"eps{i}", [128, 128], F32)) for i in range(2)]
    free = {}
    for tcn in range(8):
        bi = tcn % 2
        tg = (T_LO // 128) + tcn
        t_x = bd.op("sp", _dma(xep[bi], x_in[tg * 128:(tg + 1) * 128, :]),
                    reads=[free.get(("xread", bi))], inc="qS")
        t_o = None
        for dc in range(DC):
            pbi = dc % 2
            t_tr = bd.op("pe", lambda e, dc=dc, tcn=tcn, pbi=pbi: e.transpose(
                eps_[pbi][:], YA(dc, tcn * 128, (tcn + 1) * 128), idf[:]),
                reads=t_moe + [free.get(("ebank", pbi))], inc="pe")
            t_o = bd.op("dve", lambda e, dc=dc, bi=bi, pbi=pbi: e.tensor_add(
                osb[bi][:, dc * 128:(dc + 1) * 128], eps_[pbi][:],
                xep[bi][:, dc * 128:(dc + 1) * 128]),
                reads=[t_tr, t_x, free.get(("osbread", bi))], inc="dve")
            bd.fence("dve", t_o)
            free[("ebank", pbi)] = t_o
        free[("xread", bi)] = t_o
        t_st = bd.op("sp", _dma(out[tcn * 128:(tcn + 1) * 128, :], osb[bi][:]),
                     reads=[t_o], inc="qS")
        free[("osbread", bi)] = t_st
    bd.ops.append(Op("sp", None, [(bd.ctr["qS"], bd.ctr["qS"].count)], []))
    sF.close()

    with nc.allow_low_precision("bf16 operands / f32 accumulation by design"):
        with nc.Block() as block:
            bd.replay(block)
    es.close()
    return nc


def _prep(inputs):
    f = np.ascontiguousarray
    p = {}
    Wq = inputs["L0_attn_Wq"]; Wk = inputs["L0_attn_Wk"]; Wv = inputs["L0_attn_Wv"]
    Wg = inputs["L0_attn_Wg"]; Wc = inputs["L0_attn_Wc"]; bc = inputs["L0_attn_bc"]
    gw = inputs["L0_ffn_gate_w"]; gb = inputs["L0_ffn_gate_b"]
    w1 = inputs["L0_ffn_w1"]; b1 = inputs["L0_ffn_b1"]
    w2 = inputs["L0_ffn_w2"]; b2 = inputs["L0_ffn_b2"]
    w3 = inputs["L0_ffn_w3"]; b3 = inputs["L0_ffn_b3"]
    ws1 = inputs["L0_ffn_ws1"]; bs1 = inputs["L0_ffn_bs1"]
    ws2 = inputs["L0_ffn_ws2"]; bs2 = inputs["L0_ffn_bs2"]
    ws3 = inputs["L0_ffn_ws3"]; bs3 = inputs["L0_ffn_bs3"]

    def colblocks(W, nout):
        return f(W.reshape(DC, 128, nout, 128).transpose(2, 1, 0, 3)).reshape(
            nout, 128, DC * 128)

    p["wqp"] = colblocks(Wq, 16)
    p["wkp"] = colblocks(Wk, 8)
    p["wvp"] = f(Wv.reshape(8, 2, 128, 2, 512).transpose(3, 0, 2, 1, 4)).reshape(2, 8, 128, 2 * 512)
    p["wgp"] = f(Wg.reshape(DC, 128, 8).transpose(1, 0, 2)).reshape(128, DC * 8)
    p["wcp"] = colblocks(Wc, 16)
    p["gwp"] = f(gw.T.reshape(DC, 128, 16).transpose(1, 0, 2)).reshape(128, DC * 16)
    w1s = w1.reshape(E, HT1, 128, DC, 128).transpose(0, 1, 4, 3, 2).reshape(
        E, HT1, 128, DC * 128)
    w3s = w3.reshape(E, HT1, 128, DC, 128).transpose(0, 1, 4, 3, 2).reshape(
        E, HT1, 128, DC * 128)
    p["w13p"] = f(np.concatenate([w1s, w3s], axis=3))
    p["w2p"] = f(w2.reshape(E, 4, 512, HT1, 128).transpose(0, 1, 4, 3, 2)).reshape(
        E, 4, 128, HT1 * 512)
    ws1s = ws1.reshape(HTS, 128, DC, 128).transpose(0, 3, 2, 1).reshape(
        HTS, 128, DC * 128)
    ws3s = ws3.reshape(HTS, 128, DC, 128).transpose(0, 3, 2, 1).reshape(
        HTS, 128, DC * 128)
    p["ws13p"] = f(np.concatenate([ws1s, ws3s], axis=2))
    p["ws2p"] = f(ws2.reshape(4, 512, HTS, 128).transpose(0, 3, 2, 1)).reshape(
        4, 128, HTS * 512)
    p["b2m"] = f(b2)
    p["bcp"] = f(bc.reshape(16, 128).T)
    p["b1p"] = f(b1.reshape(E, HT1, 128).transpose(2, 0, 1).reshape(128, E * HT1))
    p["b3p"] = f(b3.reshape(E, HT1, 128).transpose(2, 0, 1).reshape(128, E * HT1))
    p["bs1p"] = f(bs1.reshape(HTS, 128).T)
    p["bs3p"] = f(bs3.reshape(HTS, 128).T)
    p["bs2p"] = f(bs2.reshape(16, 128).T)
    p["gbbp"] = f(np.broadcast_to(gb.reshape(1, 16), (128, 16)).copy())
    import ml_dtypes
    e8 = np.zeros((8, 8 * 128), dtype=ml_dtypes.bfloat16)
    for e in range(8):
        e8[e, e * 128:(e + 1) * 128] = 1.0
    p["esel8"] = e8
    e16 = np.zeros((16, 16 * 128), dtype=ml_dtypes.bfloat16)
    for e in range(16):
        e16[e, e * 128:(e + 1) * 128] = 1.0
    p["esel16"] = e16
    p["ones1"] = np.ones((1, 128), dtype=ml_dtypes.bfloat16)
    p["onescol"] = np.ones((128, 1), dtype=ml_dtypes.bfloat16)
    m = np.zeros((128, 4 * 512), dtype=np.float32)
    for o in range(4):
        for s in range(128):
            j0 = s + o * 128
            if j0 < 512:
                m[s, o * 512 + j0:(o + 1) * 512] = 1.0
    p["masks"] = m.astype(ml_dtypes.bfloat16)
    p["identf"] = np.eye(128, dtype=np.float32)
    cvec = np.zeros((128, 8), dtype=np.float32)
    cvec[:, 0] = EPS_LN
    cvec[:, 1] = GOFF
    cvec[:, 2] = 1.0
    cvec[:, 3] = -1.0
    cvec[:, 4] = -0.5
    cvec[:, 5] = LOG_ISQD
    cvec[:, 6] = 1e-6
    cvec[:, 7] = 5.5
    p["cvecp"] = cvec
    for k in list(p):
        if p[k].dtype == np.float64:
            p[k] = f(p[k].astype(np.float32))
    return p


_CACHE = {}


def kernel(**inputs):
    x = np.asarray(inputs["L0_attn_hidden_states"], dtype=np.float32)
    p = _prep({k: np.asarray(v) for k, v in inputs.items()
               if k != "L0_attn_hidden_states"})

    if "nc" not in _CACHE:
        _CACHE["nc"] = build_core()
    nc = _CACHE["nc"]

    in_maps = []
    for b in range(B):
        for h in range(2):
            m = dict(p)
            if h == 1:
                m["x"] = np.ascontiguousarray(x[b])
            else:
                xa = np.zeros((KV, D), dtype=np.float32)
                xa[T_LO:] = x[b, :TOWN]
                m["x"] = xa
            in_maps.append(m)

    res = run_bass_kernel_spmd(nc, in_maps, core_ids=list(range(8)))
    outp = np.empty((B, T, D), dtype=np.float32)
    for b in range(B):
        outp[b, 0:TOWN] = res.results[2 * b]["out"]
        outp[b, TOWN:] = res.results[2 * b + 1]["out"]
    return outp


# version 16
# speedup vs baseline: 1.0799x; 1.0799x over previous
import numpy as np
from contextlib import ExitStack

import concourse.bass as bass
import concourse.mybir as mybir
from concourse.bass_utils import run_bass_kernel_spmd

F32 = mybir.dt.float32
F32R = mybir.dt.float32r
BF16 = mybir.dt.bfloat16
AF = mybir.ActivationFunctionType
ALU = mybir.AluOpType
AX = mybir.AxisListType

D = 2048
DC = 16
T = 2048
B = 4
TOWN = 1024
HKV = 8
E = 16
HT1 = 11
HTS = 22
TOPK = 6
GOFF = 6.906768
EPS_LN = 1e-5
LOG_ISQD = -0.5 * float(np.log(128.0))

T_LO = 1024
KV = 2048
NSG = KV // 512
OWN_G0 = T_LO // 512
SCK = KV // 128

DEBUG = False

_OWN = {"sp": ("qS",), "pe": ("pe",), "act": ("act", "qA"),
        "dve": ("dve",), "pool": ("pool", "qG")}


class Ctr:
    def __init__(self, name, step=1):
        self.name = name
        self.step = step
        self.count = 0
        self.sem = None

    def bump(self):
        self.count += self.step
        return self.count


class Op:
    __slots__ = ("eng", "emit", "waits", "incs")

    def __init__(self, eng, emit, waits, incs):
        self.eng = eng
        self.emit = emit
        self.waits = waits
        self.incs = incs


class Builder:
    def __init__(self, nc):
        self.nc = nc
        self.ops = []
        self.ctr = {n: Ctr(n, 16 if n.startswith("q") else 1)
                    for n in ("qS", "qG", "qA", "pe", "act", "dve", "pool")}
        self.marks = {e: {} for e in _OWN}

    def op(self, eng, emit, reads=(), inc=None):
        m = self.marks[eng]
        waits = []
        for tok in reads:
            if tok is None:
                continue
            c, v = tok
            if c.name in _OWN[eng]:
                continue
            if m.get(c.name, 0) >= v:
                continue
            m[c.name] = v
            waits.append((c, v))
        incs = []
        tok = None
        if inc is not None:
            c = self.ctr[inc]
            v = c.bump()
            incs.append((c, v))
            tok = (c, v)
        self.ops.append(Op(eng, emit, waits, incs))
        return tok

    def fence(self, eng, tok):
        if tok is None:
            return
        c, v = tok
        self.marks[eng][c.name] = max(self.marks[eng].get(c.name, 0), v)
        self.ops.append(Op(eng, None, [(c, v)], []))

    def barrier(self, engines=("sp", "pe", "act", "dve", "pool")):
        toks = [(c, c.count) for c in self.ctr.values() if c.count > 0]
        for e in engines:
            m = self.marks[e]
            waits = []
            for c, v in toks:
                if c.name in _OWN[e] or m.get(c.name, 0) >= v:
                    continue
                m[c.name] = v
                waits.append((c, v))
            if waits:
                self.ops.append(Op(e, None, waits, []))

    def replay(self, block):
        prog = {e: [o for o in self.ops if o.eng == e] for e in _OWN}

        def body(engname):
            def f(eng):
                for o in prog[engname]:
                    for c, v in o.waits:
                        eng.wait_ge(c.sem, v)
                    if o.emit is not None:
                        ins = o.emit(eng)
                        for c, _v in o.incs:
                            ins.then_inc(c.sem, c.step)
            return f

        block.sync(body("sp"))
        block.tensor(body("pe"))
        block.scalar(body("act"))
        block.vector(body("dve"))
        block.gpsimd(body("pool"))


def _copy(eng_name):
    def g(e, dst, src):
        if eng_name == "act":
            return e.copy(dst, src)
        return e.tensor_copy(dst, src)
    return g


def _dma(dst_ap, src_ap):
    def f(eng):
        return eng.dma_start(out=dst_ap, in_=src_ap)
    return f


def build_core():
    nc = bass.Bass("TRN2", target_bir_lowering=False, debug=False)
    dram = {}

    def dr(name, shape, dtp, kind="ExternalInput"):
        dram[name] = nc.dram_tensor(name, list(shape), dtp, kind=kind).ap()
        return dram[name]

    x_in = dr("x", (KV, D), F32)
    wqp = dr("wqp", (16, 128, DC * 128), F32)
    wkp = dr("wkp", (8, 128, DC * 128), F32)
    wvp = dr("wvp", (2, 8, 128, 2 * 512), F32)
    wgp = dr("wgp", (128, DC * 8), F32)
    wcp = dr("wcp", (16, 128, DC * 128), F32)
    gwp = dr("gwp", (128, DC * 16), F32)
    w13p = dr("w13p", (E, HT1, 128, 2 * DC * 128), F32)
    w2p = dr("w2p", (E, 4, 128, HT1 * 512), F32)
    ws13p = dr("ws13p", (HTS, 128, 2 * DC * 128), F32)
    ws2p = dr("ws2p", (4, 128, HTS * 512), F32)
    b2m = dr("b2m", (16, D), F32)
    bcp = dr("bcp", (128, 16), F32)
    b1p = dr("b1p", (128, E * HT1), F32)
    b3p = dr("b3p", (128, E * HT1), F32)
    bs1p = dr("bs1p", (128, HTS), F32)
    bs3p = dr("bs3p", (128, HTS), F32)
    bs2p = dr("bs2p", (128, 16), F32)
    gbbp = dr("gbbp", (128, 16), F32)
    cvecp = dr("cvecp", (128, 8), F32)
    esel8 = dr("esel8", (8, 8 * 128), BF16)
    esel16 = dr("esel16", (16, 16 * 128), BF16)
    ones1 = dr("ones1", (1, 128), BF16)
    onescol = dr("onescol", (128, 1), BF16)
    masks = dr("masks", (128, 4 * 512), BF16)
    identf = dr("identf", (128, 128), F32)
    out = dr("out", (TOWN, D), F32, kind="ExternalOutput")
    if DEBUG:
        dr("dbg_xn", (128, DC * TOWN), BF16, kind="ExternalOutput")
        dr("dbg_logg", (8, KV), F32, kind="ExternalOutput")
        dr("dbg_yatt", (128, 16 * TOWN), BF16, kind="ExternalOutput")
        dr("dbg_ywc", (128, DC * TOWN), F32, kind="ExternalOutput")
        dr("dbg_cw", (16, TOWN), BF16, kind="ExternalOutput")
        dr("dbg_gfm", (8, KV), F32, kind="ExternalOutput")
        dr("dbg_lgp", (8, KV), F32, kind="ExternalOutput")

    bd = Builder(nc)
    es = ExitStack()

    def sb_(name, shape, dtp):
        return es.enter_context(nc.sbuf_tensor(name, list(shape), dtp))

    for c in bd.ctr.values():
        c.sem = es.enter_context(nc.semaphore(c.name))

    xn = sb_("xn", (128, DC * TOWN), BF16)
    logg = sb_("logg", (8, KV), F32)
    gfm = sb_("gfm", (8, KV), F32)
    cwfm = sb_("cwfm", (16, TOWN), BF16)
    cbias = sb_("cbias", (128, 16), F32)
    b1s = sb_("b1s", (128, E * HT1), F32)
    b3s = sb_("b3s", (128, E * HT1), F32)
    bs1s = sb_("bs1s", (128, HTS), F32)
    bs3s = sb_("bs3s", (128, HTS), F32)
    bs2s = sb_("bs2s", (128, 16), F32)
    b2s = sb_("b2s", (16, D), BF16)
    wgs = sb_("wgs", (128, DC * 8), BF16)
    gwsb = sb_("gwsb", (128, DC * 16), BF16)
    gbb = sb_("gbb", (128, 16), F32)
    cv = sb_("cv", (128, 8), F32)
    esl8 = sb_("esl8", (8, 8 * 128), BF16)
    esl16 = sb_("esl16", (16, 16 * 128), BF16)
    on1 = sb_("on1", (1, 128), BF16)
    onc = sb_("onc", (128, 1), BF16)
    idf = sb_("idf", (128, 128), F32)

    kvu = sb_("kvu", (128, 16384), F32)
    uni2 = sb_("uni2", (128, 24576), BF16)
    uni2f = uni2[:].bitcast(F32)
    kvb = kvu[:].bitcast(BF16)

    def KF(e_kv, a, b):
        return kvb[:, e_kv * KV + a: e_kv * KV + b]

    def VT(sc, a, b):
        return kvb[:, 16384 + sc * 1024 + a: 16384 + sc * 1024 + b]

    def YA(oc, a, b):
        return kvu[:, oc * TOWN + a: oc * TOWN + b]

    def QY(h, a, b):
        return uni2[:, h * TOWN + a: h * TOWN + b]

    def HS(hofs, a, b):
        return uni2[:, hofs * TOWN + a: hofs * TOWN + b]

    tcl = []
    for d_ap, s_ap in [
        (cbias[:], bcp[:]), (b1s[:], b1p[:]), (b3s[:], b3p[:]),
        (bs1s[:], bs1p[:]), (bs3s[:], bs3p[:]), (bs2s[:], bs2p[:]),
        (gbb[:], gbbp[:]), (cv[:], cvecp[:]),
        (esl8[:], esel8[:]), (esl16[:], esel16[:]),
        (on1[:], ones1[:]), (onc[:], onescol[:]),
        (idf[:], identf[:]),
    ]:
        tcl.append(bd.op("act", _dma(d_ap, s_ap), inc="qA"))
    t_gbb = tcl[6]
    for d_ap, s_ap in [(wgs[:], wgp[:]), (gwsb[:], gwp[:]), (b2s[:], b2m[:])]:
        tcl.append(bd.op("pool", _dma(d_ap, s_ap), inc="qG"))
    tok_const = tcl[-4]
    tok_constg = tcl[-1]

    sC = ExitStack()

    def csb(name, shape, dtp):
        return sC.enter_context(nc.sbuf_tensor(name, list(shape), dtp))

    def cps(name, shape):
        return sC.enter_context(nc.psum_tensor(name, list(shape), F32))

    xti = uni2f[:, 8192:10240]
    xsq = uni2f[:, 10240:12288]
    st = csb("st", (128, 8), F32)
    xng = csb("xng", (128, DC * 512), BF16)
    wk_b = [csb(f"wk_b{i}", (128, DC * 128), BF16) for i in range(2)]
    wv_b = [csb(f"wv_b{i}", (128, 2 * 512), BF16) for i in range(2)]
    kps = [cps(f"kps{i}", (128, 512)) for i in range(2)]
    vps = [cps(f"vps{i}", (128, 512)) for i in range(4)]
    gps = cps("gps", (8, 512))

    free = {}
    own_xn_toks = []
    for g in range(NSG):
        own = g >= OWN_G0
        grp_xn = []
        for ti in range(4):
            tc = g * 4 + ti
            t_x = bd.op("sp", _dma(xti, x_in[tc * 128:(tc + 1) * 128, :]),
                        reads=[free.get("xtiread")], inc="qS")
            t_s = bd.op("dve", lambda e: e.reduce_sum(st[:, 0:1], xti, axis=AX.X),
                        reads=[t_x, free.get("stread")], inc="dve")
            bd.fence("dve", t_s)
            t_q = bd.op("act", lambda e: e.activation(
                xsq, xti, AF.Square, accum_out=st[:, 1:2]),
                reads=[t_x], inc="act")
            t1 = bd.op("dve", lambda e: e.tensor_scalar(
                st[:, 2:3], st[:, 0:1], 1.0 / D, None, ALU.mult), inc="dve")
            bd.fence("dve", t1)
            t2 = bd.op("dve", lambda e: e.tensor_scalar(
                st[:, 4:5], st[:, 1:2], 1.0 / D, None, ALU.mult),
                reads=[t_q], inc="dve")
            bd.fence("dve", t2)
            t3 = bd.op("dve", lambda e: e.scalar_tensor_tensor(
                st[:, 3:4], st[:, 2:3], -1.0, st[:, 2:3], ALU.mult, ALU.mult), inc="dve")
            bd.fence("dve", t3)
            t4 = bd.op("dve", lambda e: e.tensor_add(st[:, 4:5], st[:, 4:5], st[:, 3:4]),
                       inc="dve")
            bd.fence("dve", t4)
            t5 = bd.op("act", lambda e: e.activation(
                st[:, 5:6], st[:, 4:5], AF.Sqrt, bias=cv[:, 0:1]),
                reads=[t4, tok_const], inc="act")
            bd.fence("act", t5)
            t6 = bd.op("dve", lambda e: e.reciprocal(st[:, 6:7], st[:, 5:6]),
                       reads=[t5], inc="dve")
            bd.fence("dve", t6)
            t7 = bd.op("dve", lambda e: e.tensor_scalar(
                st[:, 7:8], st[:, 2:3], st[:, 6:7], -1.0, ALU.mult, ALU.mult), inc="dve")
            bd.fence("dve", t7)
            t_n = bd.op("dve", lambda e: e.tensor_scalar(
                xti, xti, st[:, 6:7], st[:, 7:8], ALU.mult, ALU.add), inc="dve")
            bd.fence("dve", t_n)
            free["stread"] = t_n
            if own:
                tloc = tc * 128 - T_LO
                dst = lambda dc, tloc=tloc: xn[:, dc * TOWN + tloc: dc * TOWN + tloc + 128]
            else:
                dst = lambda dc, ti=ti: xng[:, dc * 512 + ti * 128: dc * 512 + (ti + 1) * 128]
            last_tr = None
            last_ev = None
            for dc in range(DC):
                pbi = dc % 4
                t_tr = bd.op("pe", lambda e, dc=dc, pbi=pbi: e.transpose(
                    vps[pbi][0:128, 0:128], xti[:, dc * 128:(dc + 1) * 128], idf[:]),
                    reads=[t_n, tok_const, free.get(("vbank", pbi))], inc="pe")
                ev_eng = "dve" if dc % 2 == 0 else "act"
                t_ev = bd.op(ev_eng, lambda e, dc=dc, pbi=pbi, dst=dst, ev_eng=ev_eng:
                    _copy(ev_eng)(e, dst(dc), vps[pbi][0:128, 0:128]),
                    reads=[t_tr, free.get("xngread") if not own else None], inc=ev_eng)
                free[("vbank", pbi)] = t_ev
                last_tr = t_tr
                last_ev = t_ev
            free["xtiread"] = last_tr
            grp_xn.append(last_ev)
        if own:
            own_xn_toks += grp_xn

        if own:
            tloc = g * 512 - T_LO
            src = lambda dc, tloc=tloc: xn[:, dc * TOWN + tloc: dc * TOWN + tloc + 512]
        else:
            src = lambda dc: xng[:, dc * 512:(dc + 1) * 512]

        for kc in range(8):
            bi = kc % 2
            t_w = bd.op("pool", _dma(wk_b[bi][:], wkp[kc]),
                        reads=[free.get(("wkread", bi))], inc="qG")
            t_mm = None
            for dc in range(DC):
                rd = ([t_w, free.get(("kbank", bi))] + grp_xn) if dc == 0 else []
                t_mm = bd.op("pe", lambda e, bi=bi, dc=dc, src=src: e.matmul(
                    kps[bi][:], wk_b[bi][:, dc * 128:(dc + 1) * 128], src(dc),
                    start=(dc == 0), stop=(dc == DC - 1)),
                    reads=rd, inc="pe" if dc == DC - 1 else None)
            free[("wkread", bi)] = t_mm
            keng = "dve" if kc % 2 == 0 else "act"
            t_ev = bd.op(keng, lambda e, bi=bi, kc=kc, g=g, keng=keng: _copy(keng)(
                e, KF(kc, g * 512, (g + 1) * 512), kps[bi][:]),
                reads=[t_mm], inc=keng)
            free[("kbank", bi)] = t_ev

        for vh in range(2):
            t_last = [None] * 4
            for dcq in range(8):
                bi = dcq % 2
                t_w = bd.op("pool", _dma(wv_b[bi][:], wvp[vh, dcq]),
                            reads=[free.get(("wvread", bi))], inc="qG")
                for dci in range(2):
                    dc = dcq * 2 + dci
                    for sc in range(4):
                        rd = [t_w] + ([free.get(("vbank", sc)), grp_xn[sc]]
                                      if dc == 0 else [])
                        t_last[sc] = bd.op("pe",
                            lambda e, bi=bi, dc=dc, dci=dci, sc=sc, src=src: e.matmul(
                                vps[sc][:], src(dc)[:, sc * 128:(sc + 1) * 128],
                                wv_b[bi][:, dci * 512:(dci + 1) * 512],
                                start=(dc == 0), stop=(dc == DC - 1)),
                            reads=rd,
                            inc="pe" if (dc == DC - 1 or sc == 3) else None)
                free[("wvread", bi)] = t_last[3]
            for sc in range(4):
                eng = "dve" if sc % 2 == 0 else "act"
                t_ev = bd.op(eng, lambda e, sc=sc, g=g, vh=vh, eng=eng: _copy(eng)(
                    e, VT(g * 4 + sc, vh * 512, (vh + 1) * 512), vps[sc][:]),
                    reads=[t_last[sc]], inc=eng)
                free[("vbank", sc)] = t_ev

        t_mm = None
        for dc in range(DC):
            rd = ([tok_constg, free.get("gpsread")] + grp_xn) if dc == 0 else []
            t_mm = bd.op("pe", lambda e, dc=dc, src=src: e.matmul(
                gps[:], wgs[:, dc * 8:(dc + 1) * 8], src(dc),
                start=(dc == 0), stop=(dc == DC - 1)),
                reads=rd, inc="pe" if dc == DC - 1 else None)
        t_ev = bd.op("dve", lambda e, g=g: e.tensor_copy(
            gfm[:, g * 512:(g + 1) * 512], gps[:]), reads=[t_mm], inc="dve")
        free["gpsread"] = t_ev
        free[("g_ev", g)] = t_ev
        if not own:
            free["xngread"] = t_mm

    for qc in range(16):
        bi = qc % 2
        t_w = bd.op("pool", _dma(wk_b[bi][:], wqp[qc]),
                    reads=[free.get(("wkread", bi))], inc="qG")
        for th in range(2):
            pb = th
            t_mm = None
            for dc in range(DC):
                rd = ([t_w, free.get(("kbank", pb))] + own_xn_toks) if dc == 0 else []
                t_mm = bd.op("pe", lambda e, bi=bi, dc=dc, th=th, pb=pb: e.matmul(
                    kps[pb][:], wk_b[bi][:, dc * 128:(dc + 1) * 128],
                    xn[:, dc * TOWN + th * 512: dc * TOWN + (th + 1) * 512],
                    start=(dc == 0), stop=(dc == DC - 1)),
                    reads=rd, inc="pe" if dc == DC - 1 else None)
            if th == 1:
                free[("wkread", bi)] = t_mm
            qeng = "dve" if qc % 2 == 0 else "act"
            t_ev = bd.op(qeng, lambda e, pb=pb, qc=qc, th=th, qeng=qeng: _copy(qeng)(
                e, QY(qc, th * 512, (th + 1) * 512), kps[pb][:]),
                reads=[t_mm], inc=qeng)
            free[("kbank", pb)] = t_ev

    g_toks = [free[("g_ev", g)] for g in range(NSG)]
    az = xsq
    t_az = bd.op("act", lambda e: e.activation(
        az[0:8, 0:KV], gfm[:], AF.Abs, bias=cv[0:8, 1:2]),
        reads=g_toks + [tok_const], inc="act")
    bd.fence("act", t_az)
    t_e1 = bd.op("act", lambda e: e.activation(
        az[0:8, 0:KV], az[0:8, 0:KV], AF.Exp, scale=cv[0:8, 3:4]), inc="act")
    bd.fence("act", t_e1)
    t_l = bd.op("act", lambda e: e.activation(
        az[0:8, 0:KV], az[0:8, 0:KV], AF.Ln, bias=cv[0:8, 2:3]), inc="act")
    t_mz = bd.op("dve", lambda e: e.tensor_scalar(
        logg[:], gfm[:], GOFF, 0.0, ALU.add, ALU.min), reads=g_toks, inc="dve")
    bd.fence("dve", t_mz)
    t_lg = bd.op("dve", lambda e: e.tensor_sub(logg[:], logg[:], az[0:8, 0:KV]),
                 reads=[t_l], inc="dve")
    bd.fence("dve", t_lg)
    t_dlg = None
    if DEBUG:
        bd.op("sp", _dma(dram["dbg_gfm"][:], gfm[:]), reads=g_toks, inc="qS")
        t_dlg = bd.op("sp", _dma(dram["dbg_lgp"][:], logg[:]), reads=[t_lg], inc="qS")
    t_z = bd.op("dve", lambda e: e.memset(az[0:8, 0:KV], 0.0), inc="dve")
    bd.fence("dve", t_z)
    t_logg = bd.op("dve", lambda e: e.tensor_tensor_scan(
        logg[:], logg[:], az[0:8, 0:KV], 0.0, ALU.add, ALU.add),
        reads=[t_dlg], inc="dve")
    bd.fence("dve", t_logg)
    if DEBUG:
        bd.op("sp", _dma(dram["dbg_logg"][:], logg[:]), reads=[t_logg], inc="qS")
        bd.op("sp", _dma(dram["dbg_xn"][:], xn[:]), reads=own_xn_toks, inc="qS")

    sC.close()
    bd.barrier()

    sD = ExitStack()

    def dsb(name, shape, dtp):
        return sD.enter_context(nc.sbuf_tensor(name, list(shape), dtp))

    def dps(name, shape):
        return sD.enter_context(nc.psum_tensor(name, list(shape), F32))

    msk = uni2[:, 16384:18432]
    t_msk = bd.op("act", _dma(msk, masks[:]), inc="qA")
    u8 = dsb("u8", (8, 512), BF16)
    u8s = dsb("u8s", (8, 512), F32)
    wfm = dsb("wfm", (8, KV), F32)
    wtm = dsb("wtm", (128, SCK * 8), F32)
    ub = dsb("ub", (128, 5 * 512), BF16)
    a1 = [dsb(f"a1_{i}", (128, 512), BF16) for i in range(3)]
    asb = [dsb(f"asb{i}", (128, 512), BF16) for i in range(3)]
    nrm = dsb("nrm", (1, 512), F32)
    rcp = dsb("rcp", (1, 512), F32)
    rcpb = dsb("rcpb", (1, 512), BF16)
    rbs = dsb("rbs", (128, 512), BF16)
    qk_ps = [dps(f"qk_ps{i}", (128, 512)) for i in range(3)]
    y_ps = [dps(f"y_ps{i}", (128, 512)) for i in range(2)]
    n_ps = [dps(f"n_ps{i}", (128, 512)) for i in range(2)]
    aux_ps = dps("aux_ps", (128, 512))

    free = {}
    for tb in range(2):
        t0g = T_LO + tb * 512
        nsc = (t0g + 512) // 128
        t_u8s = bd.op("dve", lambda e, t0g=t0g: e.tensor_scalar(
            u8s[:], logg[:, t0g:t0g + 512], logg[:, t0g:t0g + 1], None, ALU.subtract),
            reads=[t_logg, free.get("u8read")], inc="dve")
        t_u8 = bd.op("act", lambda e: e.activation(u8[:], u8s[:], AF.Exp),
                     reads=[t_u8s], inc="act")
        t_wf = bd.op("dve", lambda e, t0g=t0g, nsc=nsc: e.tensor_scalar(
            wfm[:, 0:nsc * 128], logg[:, 0:nsc * 128], logg[:, t0g:t0g + 1], None,
            ALU.subtract),
            reads=[t_logg, free.get("wfmread")], inc="dve")
        t_wfe = bd.op("act", lambda e, nsc=nsc: e.activation(
            wfm[:, 0:nsc * 128], wfm[:, 0:nsc * 128], AF.Exp,
            scale=cv[0:8, 4:5], bias=cv[0:8, 5:6]),
            reads=[t_wf, tok_const], inc="act")
        bd.fence("act", t_wfe)
        t_wt_all = []
        for sc in range(nsc):
            pbi = sc % 2
            t_tr = bd.op("pe", lambda e, sc=sc, pbi=pbi: e.transpose(
                qk_ps[pbi][0:128, 0:8], wfm[:, sc * 128:(sc + 1) * 128], idf[0:8, 0:8]),
                reads=[t_wfe, free.get(("qkread", pbi))], inc="pe")
            t_ev = bd.op("dve", lambda e, sc=sc, pbi=pbi: e.tensor_copy(
                wtm[:, sc * 8:(sc + 1) * 8], qk_ps[pbi][0:128, 0:8]),
                reads=[t_tr, free.get("wtmread")], inc="dve")
            free[("qkread", pbi)] = t_ev
            t_wt_all.append(t_ev)
        free["wfmread"] = t_wt_all[-1]

        for e_kv in range(8):
            t_sel = bd.op("pe", lambda e, e_kv=e_kv: e.matmul(
                aux_ps[:], esl8[:, e_kv * 128:(e_kv + 1) * 128], u8[:],
                start=True, stop=True),
                reads=[t_u8, tok_const, free.get("auxread")], inc="pe")
            t_ub = bd.op("dve", lambda e: e.tensor_copy(ub[:, 4 * 512:5 * 512], aux_ps[:]),
                         reads=[t_sel], inc="dve")
            free["auxread"] = t_ub
            t_ubm = [None] * 4
            for o in range(4):
                t_ubm[o] = bd.op("dve", lambda e, o=o: e.tensor_mul(
                    ub[:, o * 512:(o + 1) * 512], ub[:, 4 * 512:5 * 512],
                    msk[:, o * 512:(o + 1) * 512]),
                    reads=[t_msk], inc="dve")
            bd.fence("dve", t_ubm[3])
            t_y_stop = [None, None]
            t_n_stop = [None, None]
            NU = 2 * nsc

            def unit(n):
                sc, hh = n // 2, n % 2
                diag = sc * 128 >= t0g
                v_ub = (sc * 128 - t0g) // 128 if diag else 4
                return sc, hh, v_ub, (t_ubm[v_ub] if diag else t_ub)

            def emit_qk(n):
                sc, hh, v_ub, t_ubv = unit(n)
                h = 2 * e_kv + hh
                qbi = n % 3
                rd = [free.get(("qkread", qbi))]
                if n == 0:
                    rd += [free.get(("yread", 0)), free.get(("yread", 1)),
                           free.get(("nread", 0)), free.get(("nread", 1))]
                return bd.op("pe", lambda e, e_kv=e_kv, sc=sc, h=h, tb=tb, qbi=qbi: e.matmul(
                    qk_ps[qbi][:], KF(e_kv, sc * 128, (sc + 1) * 128),
                    QY(h, tb * 512, (tb + 1) * 512),
                    start=True, stop=True),
                    reads=rd, inc="pe")

            t_qk_l = [None] * NU
            for n in range(min(3, NU)):
                t_qk_l[n] = emit_qk(n)
            for n in range(NU):
                sc, hh, v_ub, t_ubv = unit(n)
                qbi = n % 3
                t_a1 = bd.op("act", lambda e, sc=sc, e_kv=e_kv, qbi=qbi: e.activation(
                    a1[qbi][:], qk_ps[qbi][:], AF.Square,
                    scale=wtm[:, sc * 8 + e_kv: sc * 8 + e_kv + 1]),
                    reads=[t_qk_l[n], t_wt_all[sc], free.get(("a1read", qbi))], inc="act")
                free[("qkread", qbi)] = t_a1
                t_a = bd.op("dve", lambda e, qbi=qbi, v_ub=v_ub: e.tensor_mul(
                    asb[qbi][:], a1[qbi][:], ub[:, v_ub * 512:(v_ub + 1) * 512]),
                    reads=[t_a1, t_ubv, free.get(("aread", qbi))], inc="dve")
                free[("a1read", qbi)] = t_a
                t_pv = bd.op("pe", lambda e, sc=sc, e_kv=e_kv, hh=hh, qbi=qbi, nsc=nsc: e.matmul(
                    y_ps[hh][:], VT(sc, e_kv * 128, (e_kv + 1) * 128),
                    asb[qbi][:], start=(sc == 0), stop=(sc == nsc - 1)),
                    reads=[t_a], inc="pe" if sc == nsc - 1 else None)
                t_nm = bd.op("pe", lambda e, hh=hh, qbi=qbi, sc=sc, nsc=nsc: e.matmul(
                    n_ps[hh][0:1, :], onc[:], asb[qbi][:],
                    start=(sc == 0), stop=(sc == nsc - 1)),
                    reads=[], inc="pe" if sc == nsc - 1 else None)
                free[("aread", qbi)] = t_nm if t_nm is not None else t_pv
                if n + 3 < NU:
                    t_qk_l[n + 3] = emit_qk(n + 3)
                if sc == nsc - 1:
                    t_y_stop[hh] = t_pv
                    t_n_stop[hh] = t_nm
            for hh in range(2):
                h = 2 * e_kv + hh
                t_na = bd.op("dve", lambda e, hh=hh: e.tensor_scalar(
                    nrm[:], n_ps[hh][0:1, :], 1e-6, None, ALU.add),
                    reads=[t_n_stop[hh]], inc="dve")
                bd.fence("dve", t_na)
                t_rc = bd.op("act", lambda e: e.activation(rcp[:], nrm[:], AF.Ln),
                             reads=[t_na, free.get("rcpread")], inc="act")
                bd.fence("act", t_rc)
                t_rcb = bd.op("act", lambda e: e.activation(
                    rcpb[:], rcp[:], AF.Exp, scale=cv[0:1, 3:4]), inc="act")
                bd.fence("act", t_rcb)
                t_bc = bd.op("pe", lambda e, hh=hh: e.matmul(
                    n_ps[hh][:], on1[:], rcpb[:], start=True, stop=True),
                    reads=[t_rcb, t_na], inc="pe")
                free["rcpread"] = t_bc
                t_rb = bd.op("dve", lambda e, hh=hh: e.tensor_copy(rbs[:], n_ps[hh][:]),
                             reads=[t_bc], inc="dve")
                bd.fence("dve", t_rb)
                free[("nread", hh)] = t_rb
                t_y = bd.op("dve", lambda e, h=h, tb=tb, hh=hh: e.tensor_mul(
                    QY(h, tb * 512, (tb + 1) * 512), y_ps[hh][:], rbs[:]),
                    reads=[t_y_stop[hh]], inc="dve")
                free[("yread", hh)] = t_y
        free["u8read"] = free[("yread", 1)]
        free["wtmread"] = free[("a1read", 1)]
    t_yatt = [free[("yread", 0)], free[("yread", 1)]]
    if DEBUG:
        bd.op("sp", _dma(dram["dbg_yatt"][:], uni2[:, 0:16 * TOWN]),
              reads=t_yatt, inc="qS")

    sD.close()
    bd.barrier()

    sE = ExitStack()

    def esb(name, shape, dtp):
        return sE.enter_context(nc.sbuf_tensor(name, list(shape), dtp))

    def eps(name, shape):
        return sE.enter_context(nc.psum_tensor(name, list(shape), F32))

    sE1 = ExitStack()
    wc_b = [sE1.enter_context(nc.sbuf_tensor(f"wc_b{i}", [128, DC * 128], BF16))
            for i in range(2)]
    c_ps = [eps(f"c_ps{i}", (128, 512)) for i in range(2)]
    free = {}
    for oc in range(16):
        bi = oc % 2
        t_w = bd.op("pool", _dma(wc_b[bi][:], wcp[oc]),
                    reads=[free.get(("wcread", bi))], inc="qG")
        for th in range(2):
            pb = th
            t_mm = None
            for dc in range(DC):
                rd = ([t_w, free.get(("cbank", pb))] + t_yatt) if dc == 0 else []
                t_mm = bd.op("pe", lambda e, bi=bi, dc=dc, th=th, pb=pb: e.matmul(
                    c_ps[pb][:], wc_b[bi][:, dc * 128:(dc + 1) * 128],
                    QY(dc, th * 512, (th + 1) * 512),
                    start=(dc == 0), stop=(dc == DC - 1)),
                    reads=rd, inc="pe" if dc == DC - 1 else None)
            if th == 1:
                free[("wcread", bi)] = t_mm
            t_ev = bd.op("dve", lambda e, pb=pb, oc=oc, th=th: e.tensor_scalar(
                YA(oc, th * 512, (th + 1) * 512),
                c_ps[pb][:], cbias[:, oc:oc + 1], None, ALU.add),
                reads=[t_mm, tok_const], inc="dve")
            free[("cbank", pb)] = t_ev
    t_ywc = [free[("cbank", 0)], free[("cbank", 1)]]
    if DEBUG:
        bd.op("sp", _dma(dram["dbg_ywc"][:], kvu[:]), reads=t_ywc, inc="qS")

    bd.barrier()
    sE1.close()
    r_ps = c_ps[0]
    prb = esb("prb", (128, 64), F32)
    rst = esb("rst", (128, 8), F32)
    cw_tm = esb("cw_tm", (128, 16), F32)
    free["rps_war"] = t_ywc[0]
    t_cw_tr = []
    for tcn in range(8):
        t_mm = None
        for dc in range(DC):
            rd = ([free.get("rps_war"), free.get("rps_war2")] + own_xn_toks) \
                if dc == 0 else []
            t_mm = bd.op("pe", lambda e, dc=dc, tcn=tcn: e.matmul(
                r_ps[0:128, 16:32],
                xn[:, dc * TOWN + tcn * 128: dc * TOWN + (tcn + 1) * 128],
                gwsb[:, dc * 16:(dc + 1) * 16],
                start=(dc == 0), stop=(dc == DC - 1)),
                reads=rd, inc="pe" if dc == DC - 1 else None)
        t_sc = t_mm
        t_mx = bd.op("dve", lambda e: e.reduce_max(
            rst[:, 0:1], r_ps[0:128, 16:32], axis=AX.X),
            reads=[t_sc, free.get("rstread")], inc="dve")
        bd.fence("dve", t_mx)
        t_sub = bd.op("dve", lambda e: e.tensor_scalar(
            prb[:, 48:64], r_ps[0:128, 16:32], rst[:, 0:1], None, ALU.subtract),
            reads=[free.get("prbread")], inc="dve")
        bd.fence("dve", t_sub)
        free["rps_war"] = t_sub
        t_ex = bd.op("act", lambda e: e.activation(prb[:, 0:16], prb[:, 48:64], AF.Exp),
                     reads=[t_sub], inc="act")
        t_sm = bd.op("dve", lambda e: e.reduce_sum(rst[:, 1:2], prb[:, 0:16], axis=AX.X),
                     reads=[t_ex], inc="dve")
        bd.fence("dve", t_sm)
        t_rc = bd.op("dve", lambda e: e.reciprocal(rst[:, 2:3], rst[:, 1:2]), inc="dve")
        bd.fence("dve", t_rc)
        t_pr = bd.op("dve", lambda e: e.tensor_scalar(
            prb[:, 0:16], prb[:, 0:16], rst[:, 2:3], None, ALU.mult), inc="dve")
        bd.fence("dve", t_pr)
        t_bi = bd.op("dve", lambda e: e.tensor_add(prb[:, 16:32], prb[:, 0:16], gbb[:]),
                     reads=[t_gbb], inc="dve")
        bd.fence("dve", t_bi)
        last_cw = None
        for ee in range(E):
            t_gt = bd.op("dve", lambda e, ee=ee: e.tensor_scalar(
                prb[:, 48:64], prb[:, 16:32], prb[:, 16 + ee:17 + ee], 0.0,
                ALU.is_gt, ALU.add, accum_out=rst[:, 3:4]), inc="dve")
            bd.fence("dve", t_gt)
            if ee > 0:
                t_eq = bd.op("dve", lambda e, ee=ee: e.tensor_scalar(
                    prb[:, 48:48 + ee], prb[:, 16:16 + ee], prb[:, 16 + ee:17 + ee], 0.0,
                    ALU.is_equal, ALU.add, accum_out=rst[:, 4:5]), inc="dve")
                bd.fence("dve", t_eq)
                t_rk = bd.op("dve", lambda e: e.tensor_add(
                    rst[:, 3:4], rst[:, 3:4], rst[:, 4:5]), inc="dve")
                bd.fence("dve", t_rk)
            t_ms = bd.op("dve", lambda e: e.tensor_scalar(
                rst[:, 5:6], rst[:, 3:4], float(TOPK) - 0.5, None, ALU.is_lt), inc="dve")
            bd.fence("dve", t_ms)
            last_cw = bd.op("dve", lambda e, ee=ee: e.tensor_scalar(
                cw_tm[:, ee:ee + 1], prb[:, ee:ee + 1], rst[:, 5:6], None, ALU.mult),
                reads=[free.get("cwtmread")], inc="dve")
            bd.fence("dve", last_cw)
        free["prbread"] = last_cw
        free["rstread"] = last_cw
        t_tr = bd.op("pe", lambda e: e.transpose(r_ps[0:16, 32:160], cw_tm[:], idf[:]),
                     reads=[last_cw, free.get("rps_war3")], inc="pe")
        free["cwtmread"] = t_tr
        t_cf = bd.op("dve", lambda e, tcn=tcn: e.tensor_copy(
            cwfm[0:16, tcn * 128:(tcn + 1) * 128], r_ps[0:16, 32:160]),
            reads=[t_tr], inc="dve")
        free["rps_war3"] = t_cf
        free["rps_war2"] = t_cf
        t_cw_tr.append(t_cf)
    if DEBUG:
        bd.op("sp", _dma(dram["dbg_cw"][:], cwfm[:]),
              reads=t_cw_tr, inc="qS")

    w13_b = [esb(f"w13_{i}", (128, 2 * DC * 128), BF16) for i in range(2)]
    w2_b = [esb(f"w2_{i}", (128, 6 * 512), BF16) for i in range(2)]
    ssb = [esb(f"ssb{i}", (128, 512), F32)[:] for i in range(2)]
    cwb = uni2[:, 22528:24576]
    h_ps = [eps(f"h_ps{i}", (128, 512)) for i in range(2)]
    w2_ps = [eps(f"w2_ps{i}", (128, 512)) for i in range(4)]

    state = {"cnt": 0, "scnt": 0, "w2cnt": 0}
    for k in range(HTS):
        free[("hread", k)] = t_ywc[1]

    def h_chunk(lhs1, b1col, b3col, cw_j, hofs):
        bi1 = state["cnt"] % 2
        state["cnt"] += 1
        t_w1 = bd.op("pool", _dma(w13_b[bi1][:], lhs1),
                     reads=[free.get(("w13read", bi1))], inc="qG")
        toks = []
        for tt in range(2):
            t_m1 = None
            for dc in range(DC):
                rd = ([t_w1, free.get(("hbank", 0))] + own_xn_toks) if dc == 0 else []
                t_m1 = bd.op("pe", lambda e, dc=dc, bi1=bi1, tt=tt: e.matmul(
                    h_ps[0][:], w13_b[bi1][:, dc * 128:(dc + 1) * 128],
                    xn[:, dc * TOWN + tt * 512: dc * TOWN + (tt + 1) * 512],
                    start=(dc == 0), stop=(dc == DC - 1)),
                    reads=rd, inc="pe" if dc == DC - 1 else None)
            t_m3 = None
            for dc in range(DC):
                rd = [t_w1, free.get(("hbank", 1))] if dc == 0 else []
                t_m3 = bd.op("pe", lambda e, dc=dc, bi1=bi1, tt=tt: e.matmul(
                    h_ps[1][:], w13_b[bi1][:, DC * 128 + dc * 128: DC * 128 + (dc + 1) * 128],
                    xn[:, dc * TOWN + tt * 512: dc * TOWN + (tt + 1) * 512],
                    start=(dc == 0), stop=(dc == DC - 1)),
                    reads=rd, inc="pe" if dc == DC - 1 else None)
            if tt == 1:
                free[("w13read", bi1)] = t_m3
            sbi = state["scnt"] % 2
            state["scnt"] += 1
            t_si = bd.op("act", lambda e, sbi=sbi, b1col=b1col: e.activation(
                ssb[sbi], h_ps[0][:], AF.Silu, bias=b1col),
                reads=[t_m1, free.get(("ssbread", sbi))], inc="act")
            free[("hbank", 0)] = t_si
            if cw_j is not None:
                t_cw = bd.op("dve", lambda e, sbi=sbi, cw_j=cw_j, tt=tt: e.tensor_mul(
                    ssb[sbi], ssb[sbi],
                    cwb[:, cw_j * 1024 + tt * 512: cw_j * 1024 + (tt + 1) * 512]),
                    reads=[t_si, free.get(("cwbread", cw_j))], inc="dve")
                bd.fence("dve", t_cw)
                t_s = t_cw
            else:
                t_s = t_si
            t_h = bd.op("dve", lambda e, sbi=sbi, b3col=b3col, hofs=hofs, tt=tt:
                        e.scalar_tensor_tensor(
                            HS(hofs, tt * 512, (tt + 1) * 512), h_ps[1][:], b3col,
                            ssb[sbi], ALU.add, ALU.mult),
                        reads=[t_m3, t_s, free.get(("hread", hofs))], inc="dve")
            free[("hbank", 1)] = t_h
            free[("ssbread", sbi)] = t_h
            toks.append(t_h)
        free[("hread", hofs)] = None
        return toks

    def w2_pass(members, nht, w2_dram, bs2_add, extra_mm):
        for dg in range(4):
            for tt in range(2):
                t_stop = [None] * 4
                mi = 0
                if extra_mm:
                    for di in range(4):
                        t_stop[di] = bd.op("pe", lambda e, dg=dg, di=di, tt=tt: e.matmul(
                            w2_ps[di][:],
                            b2s[:, (dg * 4 + di) * 128:(dg * 4 + di + 1) * 128],
                            cwfm[:, tt * 512:(tt + 1) * 512], start=True, stop=False),
                            reads=[free.get(("w2bank", di))] + t_cw_tr, inc=None)
                    mi = 1
                for j, (e_idx, h_toks) in enumerate(members):
                    for ht0 in range(0, nht, 6):
                        ht1 = min(ht0 + 6, nht)
                        bi = state["w2cnt"] % 2
                        state["w2cnt"] += 1
                        t_w = bd.op("pool", _dma(
                            w2_b[bi][:, 0:(ht1 - ht0) * 512],
                            w2_dram(e_idx, dg)[:, ht0 * 512:ht1 * 512]),
                            reads=[free.get(("w2read", bi))], inc="qG")
                        for ht in range(ht0, ht1):
                            last_grp = (j == len(members) - 1) and (ht == nht - 1)
                            for di in range(4):
                                rd = [t_w, h_toks[ht][tt]]
                                if mi == 0:
                                    rd.append(free.get(("w2bank", di)))
                                t_stop[di] = bd.op("pe",
                                    lambda e, bi=bi, di=di, j=j, ht=ht, ht0=ht0, mi=mi,
                                           nht=nht, tt=tt, last_grp=last_grp: e.matmul(
                                        w2_ps[di][:],
                                        w2_b[bi][:, (ht - ht0) * 512 + di * 128:
                                                 (ht - ht0) * 512 + (di + 1) * 128],
                                        HS(j * nht + ht, tt * 512, (tt + 1) * 512),
                                        start=(mi == 0), stop=last_grp),
                                    reads=rd,
                                    inc="pe" if (last_grp or di == 3) else None)
                            mi += 1
                        free[("w2read", bi)] = t_stop[3]
                for di in range(4):
                    dchunk = dg * 4 + di
                    ysl = YA(dchunk, tt * 512, (tt + 1) * 512)
                    if bs2_add:
                        t_ev = bd.op("dve", lambda e, di=di, ysl=ysl, dchunk=dchunk:
                                     e.scalar_tensor_tensor(
                                         ysl, w2_ps[di][:], bs2s[:, dchunk:dchunk + 1],
                                         ysl, ALU.add, ALU.add),
                                     reads=[t_stop[di]] + t_ywc, inc="dve")
                    else:
                        t_ev = bd.op("dve", lambda e, di=di, ysl=ysl: e.tensor_add(
                            ysl, w2_ps[di][:], ysl),
                            reads=[t_stop[di]] + t_ywc, inc="dve")
                    free[("w2bank", di)] = t_ev

    for grp in range(8):
        members = []
        for j in range(2):
            e_idx = grp * 2 + j
            t_cwb = None
            for tt in range(2):
                t_sel = bd.op("pe", lambda e, e_idx=e_idx, tt=tt: e.matmul(
                    h_ps[0][:], esl16[:, e_idx * 128:(e_idx + 1) * 128],
                    cwfm[:, tt * 512:(tt + 1) * 512], start=True, stop=True),
                    reads=[tok_const, free.get(("hbank", 0))] + t_cw_tr[4 * tt:4 * tt + 4],
                    inc="pe")
                t_cwb = bd.op("dve", lambda e, j=j, tt=tt: e.tensor_copy(
                    cwb[:, j * 1024 + tt * 512: j * 1024 + (tt + 1) * 512], h_ps[0][:]),
                    reads=[t_sel, free.get(("cwbread", j))], inc="dve")
                free[("hbank", 0)] = t_cwb
                bd.fence("dve", t_cwb)
            free[("cwbread", j)] = t_cwb
            h_toks = []
            for ht in range(HT1):
                hofs = j * HT1 + ht
                toks = h_chunk(w13p[e_idx, ht],
                               b1s[:, e_idx * HT1 + ht: e_idx * HT1 + ht + 1],
                               b3s[:, e_idx * HT1 + ht: e_idx * HT1 + ht + 1],
                               j, hofs)
                h_toks.append(toks)
            free[("cwbread", j)] = h_toks[-1][-1]
            members.append((e_idx, h_toks))
        w2_pass(members, HT1, lambda e_idx, dg: w2p[e_idx, dg],
                bs2_add=False, extra_mm=False)
        for k in range(2 * HT1):
            free[("hread", k)] = free[("w2bank", 3)]

    h_toks = []
    for ht in range(HTS):
        toks = h_chunk(ws13p[ht], bs1s[:, ht:ht + 1], bs3s[:, ht:ht + 1],
                       None, ht)
        h_toks.append(toks)
    w2_pass([(0, h_toks)], HTS, lambda e_idx, dg: ws2p[dg],
            bs2_add=True, extra_mm=True)

    t_moe = [free[("w2bank", di)] for di in range(4)]
    sE.close()
    bd.barrier()

    sF = ExitStack()
    xep = [uni2f[:, 8192:10240], uni2f[:, 10240:12288]]
    osb = [sF.enter_context(nc.sbuf_tensor(f"osb{i}", [128, D], F32)) for i in range(2)]
    eps_ = [sF.enter_context(nc.psum_tensor(f"eps{i}", [128, 128], F32)) for i in range(2)]
    free = {}
    for tcn in range(8):
        bi = tcn % 2
        tg = (T_LO // 128) + tcn
        t_x = bd.op("sp", _dma(xep[bi], x_in[tg * 128:(tg + 1) * 128, :]),
                    reads=[free.get(("xread", bi))], inc="qS")
        t_o = None
        for dc in range(DC):
            pbi = dc % 2
            t_tr = bd.op("pe", lambda e, dc=dc, tcn=tcn, pbi=pbi: e.transpose(
                eps_[pbi][:], YA(dc, tcn * 128, (tcn + 1) * 128), idf[:]),
                reads=t_moe + [free.get(("ebank", pbi))], inc="pe")
            t_o = bd.op("dve", lambda e, dc=dc, bi=bi, pbi=pbi: e.tensor_add(
                osb[bi][:, dc * 128:(dc + 1) * 128], eps_[pbi][:],
                xep[bi][:, dc * 128:(dc + 1) * 128]),
                reads=[t_tr, t_x, free.get(("osbread", bi))], inc="dve")
            bd.fence("dve", t_o)
            free[("ebank", pbi)] = t_o
        free[("xread", bi)] = t_o
        t_st = bd.op("sp", _dma(out[tcn * 128:(tcn + 1) * 128, :], osb[bi][:]),
                     reads=[t_o], inc="qS")
        free[("osbread", bi)] = t_st
    bd.ops.append(Op("sp", None, [(bd.ctr["qS"], bd.ctr["qS"].count)], []))
    sF.close()

    with nc.allow_low_precision("bf16 operands / f32 accumulation by design"):
        with nc.Block() as block:
            bd.replay(block)
    es.close()
    return nc


def _prep(inputs):
    f = np.ascontiguousarray
    p = {}
    Wq = inputs["L0_attn_Wq"]; Wk = inputs["L0_attn_Wk"]; Wv = inputs["L0_attn_Wv"]
    Wg = inputs["L0_attn_Wg"]; Wc = inputs["L0_attn_Wc"]; bc = inputs["L0_attn_bc"]
    gw = inputs["L0_ffn_gate_w"]; gb = inputs["L0_ffn_gate_b"]
    w1 = inputs["L0_ffn_w1"]; b1 = inputs["L0_ffn_b1"]
    w2 = inputs["L0_ffn_w2"]; b2 = inputs["L0_ffn_b2"]
    w3 = inputs["L0_ffn_w3"]; b3 = inputs["L0_ffn_b3"]
    ws1 = inputs["L0_ffn_ws1"]; bs1 = inputs["L0_ffn_bs1"]
    ws2 = inputs["L0_ffn_ws2"]; bs2 = inputs["L0_ffn_bs2"]
    ws3 = inputs["L0_ffn_ws3"]; bs3 = inputs["L0_ffn_bs3"]

    def colblocks(W, nout):
        return f(W.reshape(DC, 128, nout, 128).transpose(2, 1, 0, 3)).reshape(
            nout, 128, DC * 128)

    p["wqp"] = colblocks(Wq, 16)
    p["wkp"] = colblocks(Wk, 8)
    p["wvp"] = f(Wv.reshape(8, 2, 128, 2, 512).transpose(3, 0, 2, 1, 4)).reshape(2, 8, 128, 2 * 512)
    p["wgp"] = f(Wg.reshape(DC, 128, 8).transpose(1, 0, 2)).reshape(128, DC * 8)
    p["wcp"] = colblocks(Wc, 16)
    p["gwp"] = f(gw.T.reshape(DC, 128, 16).transpose(1, 0, 2)).reshape(128, DC * 16)
    w1s = w1.reshape(E, HT1, 128, DC, 128).transpose(0, 1, 4, 3, 2).reshape(
        E, HT1, 128, DC * 128)
    w3s = w3.reshape(E, HT1, 128, DC, 128).transpose(0, 1, 4, 3, 2).reshape(
        E, HT1, 128, DC * 128)
    p["w13p"] = f(np.concatenate([w1s, w3s], axis=3))
    p["w2p"] = f(w2.reshape(E, 4, 512, HT1, 128).transpose(0, 1, 4, 3, 2)).reshape(
        E, 4, 128, HT1 * 512)
    ws1s = ws1.reshape(HTS, 128, DC, 128).transpose(0, 3, 2, 1).reshape(
        HTS, 128, DC * 128)
    ws3s = ws3.reshape(HTS, 128, DC, 128).transpose(0, 3, 2, 1).reshape(
        HTS, 128, DC * 128)
    p["ws13p"] = f(np.concatenate([ws1s, ws3s], axis=2))
    p["ws2p"] = f(ws2.reshape(4, 512, HTS, 128).transpose(0, 3, 2, 1)).reshape(
        4, 128, HTS * 512)
    p["b2m"] = f(b2)
    p["bcp"] = f(bc.reshape(16, 128).T)
    p["b1p"] = f(b1.reshape(E, HT1, 128).transpose(2, 0, 1).reshape(128, E * HT1))
    p["b3p"] = f(b3.reshape(E, HT1, 128).transpose(2, 0, 1).reshape(128, E * HT1))
    p["bs1p"] = f(bs1.reshape(HTS, 128).T)
    p["bs3p"] = f(bs3.reshape(HTS, 128).T)
    p["bs2p"] = f(bs2.reshape(16, 128).T)
    p["gbbp"] = f(np.broadcast_to(gb.reshape(1, 16), (128, 16)).copy())
    import ml_dtypes
    e8 = np.zeros((8, 8 * 128), dtype=ml_dtypes.bfloat16)
    for e in range(8):
        e8[e, e * 128:(e + 1) * 128] = 1.0
    p["esel8"] = e8
    e16 = np.zeros((16, 16 * 128), dtype=ml_dtypes.bfloat16)
    for e in range(16):
        e16[e, e * 128:(e + 1) * 128] = 1.0
    p["esel16"] = e16
    p["ones1"] = np.ones((1, 128), dtype=ml_dtypes.bfloat16)
    p["onescol"] = np.ones((128, 1), dtype=ml_dtypes.bfloat16)
    m = np.zeros((128, 4 * 512), dtype=np.float32)
    for o in range(4):
        for s in range(128):
            j0 = s + o * 128
            if j0 < 512:
                m[s, o * 512 + j0:(o + 1) * 512] = 1.0
    p["masks"] = m.astype(ml_dtypes.bfloat16)
    p["identf"] = np.eye(128, dtype=np.float32)
    cvec = np.zeros((128, 8), dtype=np.float32)
    cvec[:, 0] = EPS_LN
    cvec[:, 1] = GOFF
    cvec[:, 2] = 1.0
    cvec[:, 3] = -1.0
    cvec[:, 4] = -0.5
    cvec[:, 5] = LOG_ISQD
    cvec[:, 6] = 1e-6
    cvec[:, 7] = 5.5
    p["cvecp"] = cvec
    for k in list(p):
        if p[k].dtype == np.float64:
            p[k] = f(p[k].astype(np.float32))
    return p


_CACHE = {}


def kernel(**inputs):
    x = np.asarray(inputs["L0_attn_hidden_states"], dtype=np.float32)
    p = _prep({k: np.asarray(v) for k, v in inputs.items()
               if k != "L0_attn_hidden_states"})

    if "nc" not in _CACHE:
        _CACHE["nc"] = build_core()
    nc = _CACHE["nc"]

    in_maps = []
    for b in range(B):
        for h in range(2):
            m = dict(p)
            if h == 1:
                m["x"] = np.ascontiguousarray(x[b])
            else:
                xa = np.zeros((KV, D), dtype=np.float32)
                xa[T_LO:] = x[b, :TOWN]
                m["x"] = xa
            in_maps.append(m)

    res = run_bass_kernel_spmd(nc, in_maps, core_ids=list(range(8)))
    outp = np.empty((B, T, D), dtype=np.float32)
    for b in range(B):
        outp[b, 0:TOWN] = res.results[2 * b]["out"]
        outp[b, TOWN:] = res.results[2 * b + 1]["out"]
    return outp
